# version 29
# baseline (speedup 1.0000x reference)
import contextlib
import numpy as np
import ml_dtypes
import concourse.bass as bass
import concourse.mybir as mybir
from concourse.bass_utils import run_bass_kernel_spmd

F32 = mybir.dt.float32
BF16 = mybir.dt.bfloat16
I32 = mybir.dt.int32
I8 = mybir.dt.int8
AF = mybir.ActivationFunctionType
ALU = mybir.AluOpType
AX = mybir.AxisListType

NCORES = 8
D = 2048
DF = 8192
H = 16
SEQ = 16384
LN_EPS = 1e-5
ATT_EPS = 1e-6
Q_EPS = 1e-8
QMAX = 127.0

ENGINES = ("sync", "gpsimd", "vector", "scalar", "tensor")


class Chan:

    def __init__(self, name, eng, sems):
        self.name, self.eng = name, eng
        self.sems = sems
        self.cnt = [0] * len(sems)

    def tok(self, slot=None):
        if slot is None:
            return [("C", self, s, self.cnt[s]) for s in range(len(self.sems))]
        return ("C", self, slot, self.cnt[slot])


class PB:

    def __init__(self, nc, ctx):
        self.nc = nc
        self.ops = {e: [] for e in ENGINES}
        self.esem = {e: ctx.enter_context(nc.semaphore(f"prog_{e}")) for e in ENGINES}
        self.ecount = {e: 0 for e in ENGINES}
        self.chans = []
        self.ctx = ctx
        self.barrier_toks = []

    def chan(self, name, eng, slots=1, nb=False):
        sems = [self.ctx.enter_context(self.nc.semaphore(f"ch_{name}{s}")) for s in range(slots)]
        c = Chan(name, eng, sems)
        c.nb = nb
        self.chans.append(c)
        return c

    def _w(self, dep):
        if dep is None:
            return None
        k = dep[0]
        if k == "E":
            _, eng, v = dep
            return (self.esem[eng], v) if v > 0 else None
        if k == "C":
            _, ch, s, n = dep
            return (ch.sems[s], 16 * n) if n > 0 else None
        _, sem, v = dep
        return (sem, v) if v > 0 else None

    def _waits_flat(self, deps):
        flat = []
        for d in deps:
            if isinstance(d, list):
                flat.extend(d)
            else:
                flat.append(d)
        return flat

    def _waits(self, deps):
        best = {}
        order = []
        for d in self._waits_flat(list(deps)) + self.barrier_toks:
            w = self._w(d)
            if w is not None:
                sem, val = w
                if id(sem) not in best:
                    order.append(sem)
                    best[id(sem)] = val
                else:
                    best[id(sem)] = max(best[id(sem)], val)
        return [(sem, best[id(sem)]) for sem in order]

    def op(self, eng, fn, deps=(), sig=True, incs=()):
        ii = list(incs)
        if sig:
            self.ecount[eng] += 1
            ii.append((self.esem[eng], 1))
        self.ops[eng].append((fn, self._waits(deps), ii))
        return ("E", eng, self.ecount[eng])

    def dma(self, ch, out, in_, deps=(), eng=None, slot=0, accum=None):
        ch.cnt[slot] += 1
        if accum is None:
            fn = lambda e, o=out, i=in_: e.dma_start(out=o, in_=i)
        else:
            fn = lambda e, o=out, i=in_, a=accum: e.dma_start(out=o, in_=i, accum_op=a)
        self.ops[eng or ch.eng].append((fn, self._waits(deps), [(ch.sems[slot], 16)]))
        return ch.tok(slot)

    def tok(self, eng):
        return ("E", eng, self.ecount[eng])

    def barrier(self):
        toks = [("E", e, self.ecount[e]) for e in ENGINES]
        for c in self.chans:
            if not getattr(c, "nb", False):
                toks.extend(c.tok())
        self.barrier_toks = toks

    def build(self, block):
        waited = {e: {} for e in ENGINES}

        def runner(ename):
            def run(eng):
                w8 = waited[ename]
                for fn, waits, incs in self.ops[ename]:
                    for sem, val in waits:
                        if w8.get(id(sem), 0) < val:
                            eng.wait_ge(sem, val)
                            w8[id(sem)] = val
                    inst = fn(eng)
                    for sem, amt in incs:
                        inst.then_inc(sem, amt)
            return run

        block.sync(runner("sync"))
        block.gpsimd(runner("gpsimd"))
        block.vector(runner("vector"))
        block.scalar(runner("scalar"))
        block.tensor(runner("tensor"))



def build_nc(T=2048, parts=("ffn1", "attn", "ffn2"), n_ranks=NCORES):
    nc = bass.Bass("TRN2", target_bir_lowering=False)
    NT = T // 128
    CH = min(512, T)
    NCH = T // CH
    QT = max(128, T // 4)
    NQ = T // QT
    CHQ = min(512, QT)
    NCQ = QT // CHQ
    TBLK = min(256, T)
    NTB = T // TBLK

    x0 = nc.dram_tensor("x", [D, T], F32, kind="ExternalInput").ap()
    wqT = nc.dram_tensor("wqT", [D, D], F32, kind="ExternalInput").ap()
    wkT = nc.dram_tensor("wkT", [D, D], F32, kind="ExternalInput").ap()
    wvT = nc.dram_tensor("wvT", [D, D], F32, kind="ExternalInput").ap()
    woT = nc.dram_tensor("woT", [D, D], F32, kind="ExternalInput").ap()
    wqB = nc.dram_tensor("wqB", [16, 128, D // 128, 128], F32, kind="ExternalInput").ap()
    wkB = nc.dram_tensor("wkB", [4, D // 128, 128, 512], F32, kind="ExternalInput").ap()
    wvB = nc.dram_tensor("wvB", [4, D // 128, 128, 512], F32, kind="ExternalInput").ap()
    w1W = nc.dram_tensor("w1W", [DF // 1024, D // 128, 128, 1024], F32, kind="ExternalInput").ap()
    w2W = nc.dram_tensor("w2W", [D // 1024, DF // 128, 128, 1024], F32, kind="ExternalInput").ap()
    woB = nc.dram_tensor("woB", [16, 128, D // 128, 128], F32, kind="ExternalInput").ap()
    w1T = nc.dram_tensor("w1T", [D, DF], F32, kind="ExternalInput").ap()
    w2T = nc.dram_tensor("w2T", [DF, D], F32, kind="ExternalInput").ap()
    w1s = nc.dram_tensor("w1s", [DF // n_ranks, D], F32, kind="ExternalInput").ap()
    w2s = nc.dram_tensor("w2s", [D // n_ranks, DF], F32, kind="ExternalInput").ap()
    bq_c = nc.dram_tensor("bq_c", [128, 16], F32, kind="ExternalInput").ap()
    bo_c = nc.dram_tensor("bo_c", [128, 16], F32, kind="ExternalInput").ap()
    bk_r = nc.dram_tensor("bk_r", [1, D], F32, kind="ExternalInput").ap()
    bv_r = nc.dram_tensor("bv_r", [1, D], F32, kind="ExternalInput").ap()
    b1_c = nc.dram_tensor("b1_c", [128, 64], F32, kind="ExternalInput").ap()
    b2_c = nc.dram_tensor("b2_c", [128, 16], F32, kind="ExternalInput").ap()
    pos_c = nc.dram_tensor("pos_c", [128, NT], F32, kind="ExternalInput").ap()
    pos_r = nc.dram_tensor("pos_r", [1, T], F32, kind="ExternalInput").ap()
    out = nc.dram_tensor("out", [D, T], F32, kind="ExternalOutput").ap()

    xA = nc.dram_tensor("xA", [D, T], F32).ap()
    xB = nc.dram_tensor("xB", [D, T], F32).ap()
    h_hbm = nc.dram_tensor("h_hbm", [NCH, DF // 128, 128, CH], F32).ap()
    qw1c = nc.dram_tensor("qw1c", [DF // 512, 128, D // 128, 512], BF16).ap()
    qw2c = nc.dram_tensor("qw2c", [D // 256, 128, DF // 128, 256], BF16).ap()
    xlnh = nc.dram_tensor("xlnh", [NTB, 128, D // 128, TBLK], BF16).ap()
    xlnl = nc.dram_tensor("xlnl", [NTB, 128, D // 128, TBLK], BF16).ap()
    xlnh_p = xlnh.rearrange("b p m f -> p b m f")
    xlnl_p = xlnl.rearrange("b p m f -> p b m f")
    cc_sw1i = nc.dram_tensor("cc_sw1i", [DF // n_ranks], F32).ap()
    cc_sw1o = nc.dram_tensor("cc_sw1o", [DF], F32).ap()
    cc_sw2i = nc.dram_tensor("cc_sw2i", [D // n_ranks], F32).ap()
    cc_sw2o = nc.dram_tensor("cc_sw2o", [D], F32).ap()
    cc_wui = nc.dram_tensor("cc_wui", [1, 1], F32).ap()
    cc_wuo = nc.dram_tensor("cc_wuo", [1, 1], F32).ap()
    cc_mxi = [nc.dram_tensor(f"cc_mxi{i}", [1, 1], F32).ap() for i in range(4)]
    cc_mxo = [nc.dram_tensor(f"cc_mxo{i}", [1, 1], F32).ap() for i in range(4)]
    cc_kvi = nc.dram_tensor("cc_kvi", [128, H * 129], F32).ap()
    cc_kvo = nc.dram_tensor("cc_kvo", [128, H * 129], F32).ap()
    wqhB2 = nc.dram_tensor("wqhB2", [16, 128, D], BF16).ap()
    wqlB2 = nc.dram_tensor("wqlB2", [16, 128, D], BF16).ap()
    wohB2 = nc.dram_tensor("wohB2", [16, 128, D], BF16).ap()
    wolB2 = nc.dram_tensor("wolB2", [16, 128, D], BF16).ap()
    wkhB3 = nc.dram_tensor("wkhB3", [4, 128, 16 * 512], BF16).ap()
    wklB3 = nc.dram_tensor("wklB3", [4, 128, 16 * 512], BF16).ap()
    wvhB3 = nc.dram_tensor("wvhB3", [4, 128, 16 * 512], BF16).ap()
    wvlB3 = nc.dram_tensor("wvlB3", [4, 128, 16 * 512], BF16).ap()
    rg = [list(range(n_ranks))]

    ctx = contextlib.ExitStack()
    E = ctx.enter_context
    pb = PB(nc, ctx)
    _uid = [0]
    _orig_sbuf = nc.sbuf_tensor

    def sbuf_u(name, shape, dt):
        _uid[0] += 1
        return _orig_sbuf(f"{name}_u{_uid[0]}", shape, dt)

    ones_col = E(nc.sbuf_tensor("ones_col", [128, 1], F32))
    ones_col_bf = E(nc.sbuf_tensor("ones_col_bf", [128, 1], BF16))
    ones_row = E(nc.sbuf_tensor("ones_row", [1, 128], F32))
    bqc = E(nc.sbuf_tensor("bqc", [128, 16], F32))
    boc = E(nc.sbuf_tensor("boc", [128, 16], F32))
    b1c = E(nc.sbuf_tensor("b1c", [128, 64], F32))
    b2ch = E(nc.sbuf_tensor("b2ch", [128, 16], F32))
    posc = E(nc.sbuf_tensor("posc", [128, NT], F32))
    dec_c = E(nc.sbuf_tensor("dec_c", [128, NT], F32))
    dec_r = E(nc.sbuf_tensor("dec_r", [1, T], F32))
    scw1c = E(nc.sbuf_tensor("scw1c", [128, 64], F32))
    scw2c = E(nc.sbuf_tensor("scw2c", [128, 16], F32))
    mb_ = E(nc.sbuf_tensor("mb_", [128, T], F32))
    rsb_ = E(nc.sbuf_tensor("rsb_", [128, T], F32))
    smax = E(nc.sbuf_tensor("smax", [128, 4], F32))
    smaxr = E(nc.sbuf_tensor("smaxr", [1, 128], F32))
    sval = E(nc.sbuf_tensor("sval", [1, 8], F32))
    srcol = E(nc.sbuf_tensor("srcol", [128, 2], F32))
    sxw1c = E(nc.sbuf_tensor("sxw1c", [128, 64], F32))
    sxw2c = E(nc.sbuf_tensor("sxw2c", [128, 16], F32))
    wpv = [E(nc.sbuf_tensor(f"wpv{i}", [128, 512], F32)) for i in range(2)]
    wph = [E(nc.sbuf_tensor(f"wph{i}", [128, 512], BF16)) for i in range(2)]
    wpl = [E(nc.sbuf_tensor(f"wpl{i}", [128, 512], BF16)) for i in range(2)]
    pbk = [E(nc.psum_tensor(f"pbk{i}", [128, 512], F32)) for i in range(8)]

    cc_sem = E(nc.semaphore("cc_sem"))
    cc_n = [0]

    chW = pb.chan("W", "sync", slots=4)
    chA = pb.chan("A", "scalar", slots=3)
    chH = pb.chan("Hh", "sync", slots=6)
    chQ = pb.chan("Q", "scalar", slots=2)
    chRX = pb.chan("Rx", "scalar", slots=3)
    chS = pb.chan("S", "gpsimd", slots=4)
    chM = pb.chan("M", "sync")
    chM2 = pb.chan("M2", "sync")
    chL1 = pb.chan("L1", "sync")
    chAR = pb.chan("Ar", "sync")
    chR = pb.chan("R", "sync")
    chC = pb.chan("Cc", "gpsimd")
    chWB = pb.chan("Wb", "scalar", slots=4)
    chPW = pb.chan("Pw", "sync", slots=2)
    chO = pb.chan("O", "gpsimd", slots=3)
    chPL = pb.chan("Pl", "sync", slots=4, nb=True)
    chXS = pb.chan("Xs", "sync", slots=2)
    chPP2 = pb.chan("Pp2", "sync", slots=8, nb=True)
    chPP = pb.chan("Pp", "sync", slots=8, nb=True)

    def hl_prep(slabs, chunk, ch=None, init=()):
        ch = ch or chPP
        tsp, stt = {}, {}
        for i, (src, dh, dl, sl) in enumerate(slabs):
            b = i % 2
            tl = pb.dma(chPL, wpv[b][:], src,
                        deps=([tsp[i - 2]] if i >= 2 else list(init)), slot=i % 4)
            th = pb.op("vector", lambda e, b=b: e.tensor_copy(wph[b][:], wpv[b][:]),
                       deps=[tl] + (stt[i - 2] if i - 2 in stt else list(init)))
            tsp[i] = pb.op("vector", lambda e, b=b: e.tensor_tensor(wpl[b][:], wpv[b][:], wph[b][:], ALU.subtract), deps=[th])
            s1 = pb.dma(ch, dh, wph[b][:], deps=[tsp[i]], slot=sl + 2 * b)
            s2 = pb.dma(ch, dl, wpl[b][:], deps=[tsp[i]], slot=sl + 2 * b + 1)
            stt[i] = [s1, s2]
            if (i + 1) % chunk == 0 and (i + 1) < len(slabs):
                yield
        yield

    kv_slabs = []
    for _gi in range(4):
        for (_ws, _dh3, _dl3, _sl) in ((wkB, wkhB3, wklB3, 0), (wvB, wvhB3, wvlB3, 4)):
            for _ms in range(16):
                kv_slabs.append((_ws[_gi, _ms], _dh3[_gi][:, 512 * _ms:512 * (_ms + 1)],
                                 _dl3[_gi][:, 512 * _ms:512 * (_ms + 1)], _sl))
    kvprep = {"gen": None, "tok": None}

    def collective(kind, op_, in_ap, out_ap, deps):
        cc_n[0] += 1
        pb.op("gpsimd",
              lambda e: e.collective_compute(kind, op_, replica_groups=rg,
                                             ins=[in_ap.opt()], outs=[out_ap.opt()]),
              deps=deps, sig=False, incs=[(cc_sem, 1)])
        return ("RAW", cc_sem, cc_n[0])

    t_wu0 = pb.op("vector", lambda e: e.memset(sval[:, 7:8], 0.0))
    t_wu1 = pb.dma(chC, cc_wui, sval[:, 7:8], deps=[t_wu0])
    collective("AllReduce", ALU.max, cc_wui, cc_wuo, deps=[t_wu1])
    for dst, srcap in ((bqc, bq_c), (boc, bo_c),
                       (b1c, b1_c), (b2ch, b2_c), (posc, pos_c), (dec_r, pos_r)):
        pb.dma(chM, dst[:], srcap)
    consts = chM.tok()
    pb.op("vector", lambda e: e.memset(ones_col[:], 1.0))
    pb.op("vector", lambda e: e.memset(ones_col_bf[:], 1.0))
    pb.op("vector", lambda e: e.memset(ones_row[:], 1.0))
    pb.op("vector", lambda e: e.tensor_scalar(b2ch[:], b2ch[:], 0.5, 0.0, ALU.mult, ALU.add), deps=[consts])
    pb.op("scalar", lambda e: e.activation(out=dec_c[:], in_=posc[:], func=AF.Exp, scale=-1.0 / SEQ), deps=[consts])
    pb.op("scalar", lambda e: e.activation(out=dec_r[:], in_=dec_r[:], func=AF.Exp, scale=-1.0 / SEQ), deps=[consts])

    with contextlib.ExitStack() as c2:
        E2 = c2.enter_context
        wsl = E2(nc.sbuf_tensor("wsl", [128, DF], F32))
        n1 = (DF // n_ranks) // 128
        n2 = (D // n_ranks) // 128
        wscol = E2(nc.sbuf_tensor("wscol", [128, n1 + n2], F32))
        swt = E2(nc.sbuf_tensor("swt", [64, 128], F32))
        n1 = (DF // n_ranks) // 128
        tprev, trtok = None, []
        for j in range(n1):
            tl = pb.dma(chW, wsl[:, 0:D], w1s[128 * j:128 * (j + 1), :], deps=[tprev] if tprev else [])
            tprev = pb.op("vector", lambda e, j=j: e.tensor_reduce(
                wscol[:, j:j + 1], wsl[:, 0:D], axis=AX.X, op=ALU.max, apply_absolute_value=True), deps=[tl])
            trtok.append(tprev)
        sts = [pb.dma(chC, cc_sw1i[128 * j:128 * (j + 1)], wscol[:, j:j + 1], deps=[trtok[j]]) for j in range(n1)]
        cc1 = collective("AllGather", ALU.bypass, cc_sw1i, cc_sw1o, deps=sts)
        tprev2, trtok2 = None, []
        for j in range(n2):
            tl = pb.dma(chW, wsl[:], w2s[128 * j:128 * (j + 1), :], deps=[tprev2 or trtok[-1]])
            tprev2 = pb.op("vector", lambda e, j=j: e.tensor_reduce(
                wscol[:, n1 + j:n1 + j + 1], wsl[:], axis=AX.X, op=ALU.max, apply_absolute_value=True), deps=[tl])
            trtok2.append(tprev2)
        sts2 = [pb.dma(chC, cc_sw2i[128 * j:128 * (j + 1)], wscol[:, n1 + j:n1 + j + 1], deps=[trtok2[j]]) for j in range(n2)]
        cc2 = collective("AllGather", ALU.bypass, cc_sw2i, cc_sw2o, deps=sts2)
        t_l1 = pb.dma(chL1, swt[:], cc_sw1o.rearrange("(c p) -> c p", p=128), deps=[cc1])
        tt1 = [pb.op("vector", lambda e, i=i, j=j: e.transpose(
            scw1c[32 * j:32 * j + 32, 32 * i:32 * i + 32], swt[32 * i:32 * i + 32, 32 * j:32 * j + 32]),
            deps=[t_l1]) for i in range(2) for j in range(4)]
        tt2 = [pb.dma(chM2, scw2c[:, j:j + 1], cc_sw2o[128 * j:128 * (j + 1)], deps=[cc2]) for j in range(16)]
        t_sc1 = pb.op("vector", lambda e: e.tensor_scalar(scw1c[:], scw1c[:], 1.0 / QMAX, Q_EPS, ALU.mult, ALU.max), deps=tt1)
        t_sc2 = pb.op("vector", lambda e: e.tensor_scalar(scw2c[:], scw2c[:], 1.0 / QMAX, Q_EPS, ALU.mult, ALU.max), deps=[tt2[-1]])
    pb.barrier()


    def ln_stats(src_hbm):
        with contextlib.ExitStack() as c2:
            E2 = c2.enter_context
            slab = [E2(sbuf_u(f"lns{i}", [128, T], F32)) for i in range(3)]
            sq = E2(sbuf_u("lnsq", [128, T], F32))
            sc2 = E2(sbuf_u("lnsc", [1, 4 * T], F32))
            mrow, rsrow, s1row, s2row = (sc2[:, i * T:(i + 1) * T] for i in range(4))
            tsq, tmark = None, {}
            for ft in range(16):
                b = ft % 3
                deps = [tmark[ft - 3]] if ft >= 3 else []
                tl = pb.dma(chA, slab[b][:], src_hbm[128 * ft:128 * (ft + 1), :], deps=deps, slot=b)
                tsq = pb.op("scalar", lambda e, b=b: e.activation(out=sq[:], in_=slab[b][:], func=AF.Square),
                            deps=[tl, tmark.get(ft - 1)] if ft >= 1 else [tl])
                for c in range(NCH):
                    pb.op("tensor", lambda e, b=b, c=c, ft=ft: e.matmul(
                        pbk[c][0:1, 0:CH], ones_col[:], slab[b][:, CH * c:CH * (c + 1)],
                        start=(ft == 0), stop=(ft == 15)), deps=[tl], sig=False)
                for c in range(NCH):
                    pb.op("tensor", lambda e, c=c, ft=ft: e.matmul(
                        pbk[NCH + c][0:1, 0:CH], ones_col[:], sq[:, CH * c:CH * (c + 1)],
                        start=(ft == 0), stop=(ft == 15)), deps=[tsq], sig=(c == NCH - 1))
                tmark[ft] = pb.tok("tensor")
            tmm = pb.tok("tensor")
            tm = []
            for c in range(NCH):
                tm.append(pb.op("vector", lambda e, c=c: e.tensor_scalar(
                    mrow[:, CH * c:CH * (c + 1)], pbk[c][0:1, 0:CH], 1.0 / D, 0.0, ALU.mult, ALU.add), deps=[tmm]))
                tm.append(pb.op("vector", lambda e, c=c: e.tensor_scalar(
                    s1row[:, CH * c:CH * (c + 1)], pbk[NCH + c][0:1, 0:CH], 1.0 / D, 0.0, ALU.mult, ALU.add), deps=[tmm]))
            t1 = pb.op("vector", lambda e: e.tensor_tensor(s2row[:], mrow[:], mrow[:], ALU.mult), deps=tm)
            t2 = pb.op("vector", lambda e: e.scalar_tensor_tensor(s1row[:], s1row[:], LN_EPS, s2row[:], ALU.add, ALU.subtract), deps=[t1])
            t3 = pb.op("scalar", lambda e: e.activation(out=s2row[:], in_=s1row[:], func=AF.Sqrt), deps=[t2])
            t4 = pb.op("vector", lambda e: e.reciprocal(rsrow[:], s2row[:]), deps=[t3])
            outs = []
            for c in range(NCH):
                tbm = pb.op("tensor", lambda e, c=c: e.matmul(
                    pbk[c][:, 0:CH], ones_row[:], mrow[:, CH * c:CH * (c + 1)], start=True, stop=True),
                    deps=[t4] + tm, sig=True)
                outs.append(pb.op("vector", lambda e, c=c: e.tensor_copy(mb_[:, CH * c:CH * (c + 1)], pbk[c][:, 0:CH]), deps=[tbm]))
                tbr = pb.op("tensor", lambda e, c=c: e.matmul(
                    pbk[NCH + c][:, 0:CH], ones_row[:], rsrow[:, CH * c:CH * (c + 1)], start=True, stop=True),
                    deps=[t4] + tm, sig=True)
                outs.append(pb.op("vector", lambda e, c=c: e.tensor_copy(rsb_[:, CH * c:CH * (c + 1)], pbk[NCH + c][:, 0:CH]), deps=[tbr]))
            return outs

    def scalar_allreduce_max(vslot, cc_idx, dep, filler=None):
        tsh = pb.dma(chR, smaxr[0:1, :], smax[:, vslot:vslot + 1], deps=[dep])
        tg = pb.op("vector", lambda e: e.tensor_reduce(sval[:, 4 * vslot:4 * vslot + 1] if False else sval[:, 0:1], smaxr[:], axis=AX.X, op=ALU.max), deps=[tsh])
        tci = pb.dma(chC, cc_mxi[cc_idx], sval[:, 0:1], deps=[tg])
        ctk = collective("AllReduce", ALU.max, cc_mxi[cc_idx], cc_mxo[cc_idx], deps=[tci])
        tco = pb.dma(chC, sval[:, 0:1], cc_mxo[cc_idx], deps=[ctk])
        if filler:
            filler()
        ts = pb.op("vector", lambda e: e.tensor_scalar(sval[:, 1:2], sval[:, 0:1], 1.0 / QMAX, Q_EPS, ALU.mult, ALU.max), deps=[tco])
        tr = pb.op("vector", lambda e: e.reciprocal(sval[:, 2:3], sval[:, 1:2]), deps=[ts])
        tb = pb.op("tensor", lambda e: e.matmul(pbk[7][:, 0:2], ones_row[:], sval[:, 1:3], start=True, stop=True), deps=[tr], sig=True)
        return pb.op("vector", lambda e: e.tensor_copy(srcol[:], pbk[7][:, 0:2]), deps=[tb])

    def quant_pass(src_hbm, dst_cb, stats_dep, with_scale):
        with contextlib.ExitStack() as c2:
            E2 = c2.enter_context
            slab = [E2(sbuf_u(f"qp{i}", [128, T], F32)) for i in range(2)]
            dt_ = E2(sbuf_u("qpd", [128, T], F32))
            it_ = E2(sbuf_u("qpi", [128, T], F32))
            tprev = pb.op("vector", lambda e: e.memset(smax[:, 0:1], 0.0), deps=stats_dep) if not with_scale else None
            tend = {}
            for ft in range(16):
                b = ft % 2
                tl = pb.dma(chA, slab[b][:], src_hbm[128 * ft:128 * (ft + 1), :],
                            deps=[tend[ft - 2]] if ft >= 2 else list(stats_dep), slot=b)
                d0 = [tl] + stats_dep + ([tprev] if tprev else [])
                ta = pb.op("vector", lambda e, b=b: e.tensor_tensor(dt_[:], slab[b][:], mb_[:], ALU.subtract), deps=d0)
                tb = pb.op("vector", lambda e: e.tensor_tensor(dt_[:], dt_[:], rsb_[:], ALU.mult), deps=[ta])
                if with_scale:
                    ti = pb.op("vector", lambda e: e.tensor_scalar(it_[:].bitcast(I32), dt_[:], srcol[:, 1:2], 0.0, ALU.mult, ALU.add), deps=[tb])
                    tprev = dst_cb(ft, it_, [ti])
                else:
                    tr = pb.op("vector", lambda e: e.tensor_reduce(smax[:, 1:2], dt_[:], axis=AX.X, op=ALU.max, apply_absolute_value=True), deps=[tb, tprev] if tprev else [tb])
                    tprev = pb.op("vector", lambda e: e.tensor_tensor(smax[:, 0:1], smax[:, 0:1], smax[:, 1:2], ALU.max), deps=[tr])
                tend[ft] = tprev
            return tprev, tend

    def wprep(wsrcW, ab_row, nin, nout, cache, blksz, tdep, bufs, blk_tok):
        FW = 1024
        ws, ab, yt, qtw, arow = bufs
        prev_w = []
        ty_t, tq_t = {}, {}
        nms = nin // 128
        for fw in range(nout // FW):
            f0 = FW * fw
            t_r = pb.dma(chAR, ab[:], ab_row[f0:f0 + FW].unsqueeze(0).broadcast_to([128, FW]), deps=list(tdep) + prev_w)
            t_s = pb.op("vector", lambda e: e.tensor_scalar(ab[:], ab[:], 1.0 / QMAX, Q_EPS, ALU.mult, ALU.max), deps=[t_r])
            t_ab = pb.op("vector", lambda e: e.reciprocal(ab[:], ab[:]), deps=[t_s])
            GMS = min(nms, 16)
            qv = qtw[:, 0:GMS * FW].rearrange("p (m f) -> p m f", m=GMS)
            sts = []
            for ms in range(nms):
                b4, b2 = ms % 4, ms % 2
                tl = pb.dma(chW, ws[b4][:], wsrcW[fw, ms], deps=[ty_t[ms - 3]] if ms >= 3 else [t_ab], slot=b4)
                ty_t[ms] = pb.op("vector", lambda e, b4=b4, b2=b2: e.tensor_tensor(yt[b2][:].bitcast(I32), ws[b4][:], ab[:], ALU.mult),
                                 deps=[tl, t_ab] + ([tq_t[ms - 2]] if ms >= 2 else prev_w))
                tq_t[ms] = pb.op("vector", lambda e, b2=b2, ms=ms: e.tensor_copy(qtw[:, FW * (ms % GMS):FW * (ms % GMS + 1)], yt[b2][:].bitcast(I32)),
                                 deps=[ty_t[ms]] + ([sts[-1]] if (ms % GMS == 0 and sts) else []))
                if (ms + 1) % GMS == 0:
                    g0 = ms - GMS + 1
                    for blk in range(FW // blksz):
                        gblk = (f0 + blksz * blk) // blksz
                        sts.append(pb.dma(chPW, cache[gblk, :, g0:g0 + GMS, :], qv[:, :, blksz * blk:blksz * (blk + 1)],
                                          deps=[tq_t[ms]], slot=fw % 2))
                    if ms + 1 < nms:
                        yield
            for blk in range(FW // blksz):
                blk_tok[(f0 + blksz * blk) // blksz] = [sts[-1]]
            prev_w = [sts[-1]]
            blk_tok["last"] = prev_w
            yield

    def ffn(src_hbm, dst_hbm, first, cc0, dst_is_out):
        tpre = pb.dma(chO if dst_is_out else chS, dst_hbm, src_hbm, slot=0)
        stats = ln_stats(src_hbm)
        pb.barrier()
        tmax, _ = quant_pass(src_hbm, None, stats, False)
        qx_ctx = contextlib.ExitStack()
        qx = qx_ctx.enter_context(sbuf_u("qx", [128, 16 * T], BF16))
        bt1, bt2 = {}, {}
        if first:
            FW = 1024
            E3 = qx_ctx.enter_context
            wpbufs = ([E3(sbuf_u("wp", [128, FW], F32)) for i in range(4)],
                      E3(sbuf_u("wpa", [128, FW], F32)),
                      [E3(sbuf_u("wpy", [128, FW], F32)) for i in range(2)],
                      E3(sbuf_u("wpq", [128, 16 * FW], BF16)),
                      None)
            NW1 = DF // 1024
            g1 = wprep(w1W, cc_sw1o, D, DF, qw1c, 512, [tmax], wpbufs, bt1)
            g1n = 2
            g2 = None

        def fill_w01():
            if first:
                next(g1)
                next(g1)
        tsr = scalar_allreduce_max(0, cc0, tmax, filler=fill_w01)
        tsx1 = pb.op("vector", lambda e: e.tensor_scalar(sxw1c[:], scw1c[:], srcol[:, 0:1], 0.0, ALU.mult, ALU.add), deps=[tsr])

        def qx_store(ft, it_, deps):
            return pb.op("vector", lambda e, ft=ft: e.tensor_copy(qx[:, T * ft:T * (ft + 1)], it_[:].bitcast(I32)), deps=deps)
        tqx, qx_t = quant_pass(src_hbm, qx_store, [tsr], True)
        pb.barrier()

        with contextlib.ExitStack() as c2:
            E2 = c2.enter_context
            wb = [E2(sbuf_u("p5w", [128, 16 * 512], BF16)) for i in range(2)]
            ht = [E2(sbuf_u("p5h", [128, CH], F32)) for i in range(4)]
            tz = pb.op("vector", lambda e: e.memset(smax[:, 2:3], 0.0))
            hm_prev = tz
            ev_act, ev_st = {}, {}
            NB = DF // 512
            for fb in range(NB):
                if first:
                    if g1n < NW1:
                        next(g1)
                        g1n += 1
                    else:
                        if g2 is None:
                            g2 = wprep(w2W, cc_sw2o, DF, D, qw2c, 256,
                                       bt1["last"], wpbufs, bt2)
                        next(g2, None)
                b = fb % 2
                i0 = fb * 4 * NCH
                wdep = list(bt1[fb]) if fb in bt1 else []
                if fb >= 2:
                    wdep.append(ev_act[4 * NCH * (fb - 1)])
                tw = pb.dma(chWB, wb[b][:, 0:8 * 512], qw1c[fb, :, 0:8, :], deps=wdep, slot=2 * b)
                twB = pb.dma(chWB, wb[b][:, 8 * 512:16 * 512], qw1c[fb, :, 8:16, :], deps=wdep, slot=2 * b + 1)
                for f in range(4):
                    for tc in range(NCH):
                        j = i0 + 4 * f + tc if False else i0 + NCH * f + tc
                        bank = j % 6
                        mmdep = [tw] + ([qx_t[0]] if fb == 0 else [tqx]) + \
                            ([ev_act[j - 6]] if j - 6 in ev_act else [])
                        for ms in range(16):
                            chase = [qx_t[ms]] if fb == 0 else []
                            pb.op("tensor", lambda e, b=b, f=f, ms=ms, tc=tc, bank=bank: e.matmul(
                                pbk[bank][:, 0:CH],
                                wb[b][:, 512 * ms + 128 * f:512 * ms + 128 * (f + 1)],
                                qx[:, T * ms + CH * tc:T * ms + CH * (tc + 1)],
                                start=(ms == 0), stop=(ms == 15)),
                                deps=(mmdep if ms == 0 else []) + chase + ([twB] if ms == 8 else []), sig=(ms == 15))
                        tmm = pb.tok("tensor")
                        fidx = fb * 4 + f
                        hb = j % 4
                        hdep = [ev_st.get(j - 4)] if j - 4 in ev_st else []
                        tg = pb.op("scalar", lambda e, hb=hb, bank=bank, fidx=fidx: e.activation(
                            out=ht[hb][:], in_=pbk[bank][:, 0:CH], func=AF.Gelu,
                            bias=b1c[:, fidx:fidx + 1], scale=sxw1c[:, fidx:fidx + 1]),
                            deps=[tmm, tsx1] + hdep)
                        trm = pb.op("vector", lambda e, hb=hb: e.tensor_reduce(
                            smax[:, 3:4], ht[hb][:], axis=AX.X, op=ALU.max, apply_absolute_value=True), deps=[tg, hm_prev])
                        hm_prev = pb.op("vector", lambda e: e.tensor_tensor(
                            smax[:, 2:3], smax[:, 2:3], smax[:, 3:4], ALU.max), deps=[trm, hm_prev])
                        ev_st[j] = pb.dma(chS, h_hbm[tc, fidx], ht[hb][:], deps=[trm], slot=j % 4)
                        ev_act[j] = tg
        qx_ctx.close()
        pb.barrier()
        tsr2 = scalar_allreduce_max(2, cc0 + 1, hm_prev)
        tsxa = pb.op("vector", lambda e: e.tensor_scalar(sxw2c[:], scw2c[:], srcol[:, 0:1], 0.0, ALU.mult, ALU.add), deps=[tsr2])
        tsx2 = pb.op("vector", lambda e: e.tensor_scalar(sxw2c[:], sxw2c[:], 0.5, 0.0, ALU.mult, ALU.add), deps=[tsxa])
        hst_done = chS.tok()

        with contextlib.ExitStack() as c2:
            E2 = c2.enter_context
            qh = [E2(sbuf_u("p6qh", [128, 64 * CH], BF16)) for i in range(2)]
            hb_ = [E2(sbuf_u("p6h", [128, CH], F32)) for i in range(5)]
            qsc = [E2(sbuf_u("p6qs", [128, CH], F32)) for i in range(3)]
            w2b = [E2(sbuf_u("p6w", [128, 16 * 256], BF16)) for i in range(2)]
            ot_ = [E2(sbuf_u("p6o", [128, CH], F32)) for i in range(3)]
            NF2 = DF // 128
            tg_tok = {}
            st_tok = {}
            qdone = {}
            mmtok = {}
            sub_tok = {}
            for tc in range(NCH):
                if first:
                    if kvprep["gen"] is None:
                        kvprep["gen"] = hl_prep(kv_slabs, 32, ch=chPP2)
                    next(kvprep["gen"], None)
                qb_ = tc % 2
                tqh = {}
                qfree = [mmtok[tc - 2]] if tc - 2 in mmtok else []
                for f2 in range(NF2):
                    r = f2 % 5
                    r3 = f2 % 3
                    ldep = [hst_done] + ([tqh[f2 - 5]] if f2 >= 5 else ([qdone[tc - 1]] if tc - 1 in qdone else []))
                    tl = pb.dma(chH, hb_[r][:], h_hbm[tc, f2], deps=ldep, slot=r)
                    ty = pb.op("vector", lambda e, r=r, r3=r3: e.tensor_scalar(qsc[r3][:].bitcast(I32), hb_[r][:], srcol[:, 1:2], 0.0, ALU.mult, ALU.add),
                               deps=[tl, tsr2] + (qfree if f2 == 0 else []))
                    tq = pb.op("vector", lambda e, r3=r3, f2=f2, qb_=qb_: e.tensor_copy(qh[qb_][:, CH * f2:CH * (f2 + 1)], qsc[r3][:].bitcast(I32)), deps=[ty])
                    tqh[f2] = tq
                qdone[tc] = pb.tok("vector")
                for db in range(D // 256):
                    bank0, bank1 = (2 * db) % 4, (2 * db + 1) % 4
                    for sub in range(4):
                        b = (4 * db + sub) % 2
                        k = 4 * (NCH * 0 + db) + sub + 32 * tc
                        wdep = ([sub_tok[k - 2]] if k - 2 in sub_tok else []) + \
                               (bt2[db] if (tc == 0 and sub == 0 and db in bt2) else [])
                        tw1 = pb.dma(chWB, w2b[b][:, 0:8 * 256], qw2c[db, :, 16 * sub:16 * sub + 8, :], deps=wdep, slot=2 * b)
                        tw2 = pb.dma(chWB, w2b[b][:, 8 * 256:16 * 256], qw2c[db, :, 16 * sub + 8:16 * (sub + 1), :], deps=wdep, slot=2 * b + 1)
                        for d2 in range(2):
                            dt = 2 * db + d2
                            bank = bank0 if d2 == 0 else bank1
                            pdt = dt - 4 if dt >= 4 else dt + 12
                            mmdep = [tw1, qdone[tc], tsx2] + ([tg_tok[pdt]] if pdt in tg_tok else [])
                            for msl in range(16):
                                ms = 16 * sub + msl
                                pb.op("tensor", lambda e, b=b, msl=msl, ms=ms, d2=d2, bank=bank, qb_=qb_: e.matmul(
                                    pbk[bank][:, 0:CH],
                                    w2b[b][:, 256 * msl + 128 * d2:256 * msl + 128 * (d2 + 1)],
                                    qh[qb_][:, CH * ms:CH * (ms + 1)],
                                    start=(ms == 0), stop=(ms == NF2 - 1)),
                                    deps=(mmdep if msl == 0 else []) + ([tw2] if msl == 8 else []), sig=(msl == 15))
                        sub_tok[k] = pb.tok("tensor")
                    tmm = pb.tok("tensor")
                    for d2 in range(2):
                        dt = 2 * db + d2
                        bank = bank0 if d2 == 0 else bank1
                        r = dt % 3
                        tg = pb.op("scalar", lambda e, r=r, bank=bank, dt=dt: e.activation(
                            out=ot_[r][:], in_=pbk[bank][:, 0:CH], func=AF.Identity,
                            bias=b2ch[:, dt:dt + 1], scale=sxw2c[:, dt:dt + 1]),
                            deps=[tmm] + ([st_tok[dt - 3]] if dt - 3 in st_tok else []))
                        tg_tok[dt] = tg
                        st_tok[dt] = pb.dma(chO if dst_is_out else chS,
                                            dst_hbm[128 * dt:128 * (dt + 1), CH * tc:CH * (tc + 1)], ot_[r][:],
                                            deps=[tg, tpre], slot=r, accum=ALU.add)
                mmtok[tc] = pb.tok("tensor")
            if first:
                while next(kvprep["gen"], "END") != "END":
                    pass
                kvprep["tok"] = chPP2.tok()
        pb.barrier()

    def attn(src_hbm, dst_hbm, dst_is_out):
        tpre = pb.dma(chO if dst_is_out else chS, dst_hbm, src_hbm, slot=0)
        kv_ctx = contextlib.ExitStack()
        kvbuf = kv_ctx.enter_context(sbuf_u("kvbuf", [128, H * 129], F32))
        kvh = kv_ctx.enter_context(sbuf_u("kvh", [128, H * 129], BF16))
        kvl = kv_ctx.enter_context(sbuf_u("kvl", [128, H * 129], BF16))
        stats = ln_stats(src_hbm)
        pb.barrier()
        with contextlib.ExitStack() as c2:
            E2 = c2.enter_context
            slab = [E2(sbuf_u(f"a2s{i}", [128, T], F32)) for i in range(2)]
            dt_ = E2(sbuf_u("a2d", [128, T], F32))
            hti = E2(sbuf_u("a2h", [128, T], BF16))
            lti = E2(sbuf_u("a2l", [128, T], BF16))
            tend = {}
            for ft in range(16):
                b = ft % 2
                tl = pb.dma(chA, slab[b][:], src_hbm[128 * ft:128 * (ft + 1), :],
                            deps=[tend[ft - 2]] if ft >= 2 else [], slot=b)
                prev = [tend[ft - 1]] if ft >= 1 else []
                ta = pb.op("vector", lambda e, b=b: e.tensor_tensor(dt_[:], slab[b][:], mb_[:], ALU.subtract), deps=[tl] + stats + prev)
                tb = pb.op("vector", lambda e: e.tensor_tensor(dt_[:], dt_[:], rsb_[:], ALU.mult), deps=[ta])
                th = pb.op("vector", lambda e: e.tensor_copy(hti[:], dt_[:]), deps=[tb])
                tl2 = pb.op("vector", lambda e: e.tensor_tensor(lti[:], dt_[:], hti[:], ALU.subtract), deps=[th])
                pb.dma(chXS, xlnh_p[:, :, ft, :], hti[:], deps=[th], slot=ft % 2)
                pb.dma(chXS, xlnl_p[:, :, ft, :], lti[:], deps=[tl2], slot=ft % 2)
                tend[ft] = chXS.tok(ft % 2)
        xln_done = chXS.tok()
        pb.barrier()

        with contextlib.ExitStack() as c2:
            E2 = c2.enter_context
            wkh = E2(sbuf_u("wkh", [128, 16 * 512], BF16))
            wkl = E2(sbuf_u("wkl", [128, 16 * 512], BF16))
            wvh = E2(sbuf_u("wvh", [128, 16 * 512], BF16))
            wvl = E2(sbuf_u("wvl", [128, 16 * 512], BF16))
            wtmp = [E2(sbuf_u(f"a3w{i}", [128, 512], F32)) for i in range(2)]
            wd = E2(sbuf_u("a3wd", [128, 512], F32))
            xbh = [E2(sbuf_u(f"a3xh{i}", [128, 16 * TBLK], BF16)) for i in range(2)]
            xbl = [E2(sbuf_u(f"a3xl{i}", [128, 16 * TBLK], BF16)) for i in range(2)]
            kf = [E2(sbuf_u(f"a3kf{i}", [128, 512], F32)) for i in range(2)]
            khb = [E2(sbuf_u(f"a3kh{i}", [128, 512], BF16)) for i in range(2)]
            klb = [E2(sbuf_u(f"a3kl{i}", [128, 512], BF16)) for i in range(2)]
            vf = [E2(sbuf_u(f"a3vf{i}", [128, 512], F32)) for i in range(2)]
            vhb = [E2(sbuf_u(f"a3vh{i}", [128, 512], BF16)) for i in range(2)]
            vlb = [E2(sbuf_u(f"a3vl{i}", [128, 512], BF16)) for i in range(2)]
            bkb = E2(sbuf_u("a3bkb", [128, 512], F32))
            bvb = E2(sbuf_u("a3bvb", [128, 512], F32))
            bkvr = E2(sbuf_u("a3bkvr", [1, 1024], F32))
            wqB_f = wqB.rearrange("f p m c -> f p (m c)")
            woB_f = woB.rearrange("f p m c -> f p (m c)")
            qo_slabs = []
            for (wsrcB, dh2, dl2, sl) in ((wqB_f, wqhB2, wqlB2, 0), (woB_f, wohB2, wolB2, 4)):
                for f in range(16):
                    for c in range(4):
                        qo_slabs.append((wsrcB[f][:, 512 * c:512 * (c + 1)],
                                         dh2[f][:, 512 * c:512 * (c + 1)],
                                         dl2[f][:, 512 * c:512 * (c + 1)], sl))
            if kvprep["tok"] is None:
                g0 = hl_prep(kv_slabs, 10 ** 6, ch=chPP2)
                while next(g0, "END") != "END":
                    pass
                kvprep["tok"] = chPP2.tok()
            gqo = hl_prep(qo_slabs, 2, init=[kvprep["tok"]])
            gqo_live = [True]

            def pull_qo():
                if gqo_live[0] and next(gqo, "END") == "END":
                    gqo_live[0] = False
            prev_ev, prev_kv, prev_e4 = None, None, None
            for gi in range(4):
                gs = 512 * gi
                ldep = [kvprep["tok"]] + ([prev_ev] if gi > 0 else [])
                tkh = pb.dma(chW, wkh[:], wkhB3[gi], deps=ldep, slot=0)
                tkl = pb.dma(chW, wkl[:], wklB3[gi], deps=[], slot=1)
                tvh = pb.dma(chW, wvh[:], wvhB3[gi], deps=[], slot=2)
                tvl = pb.dma(chW, wvl[:], wvlB3[gi], deps=[], slot=3)
                tbl1 = pb.dma(chM, bkvr[:, 0:512], bk_r[:, gs:gs + 512], deps=[prev_ev] if gi > 0 else [])
                tbl2 = pb.dma(chM, bkvr[:, 512:1024], bv_r[:, gs:gs + 512], deps=[])
                tb1 = pb.op("tensor", lambda e: e.matmul(pbk[6][:, 0:512], ones_row[:], bkvr[:, 0:512], start=True, stop=True), sig=True,
                            deps=[tbl1, tbl2] + ([prev_ev, prev_e4] if gi > 0 else []))
                tbk = pb.op("vector", lambda e: e.tensor_copy(bkb[:], pbk[6][:, 0:512]), deps=[tb1])
                tb2 = pb.op("tensor", lambda e: e.matmul(pbk[7][:, 0:512], ones_row[:], bkvr[:, 512:1024], start=True, stop=True), sig=True, deps=[tbk])
                tbv = pb.op("vector", lambda e: e.tensor_copy(bvb[:], pbk[7][:, 0:512]), deps=[tb2])
                ev_done = {}
                stash = {}
                kv_tok = {}

                def emit_kv(tt2):
                    pp2, e4p, f4p = stash[tt2]
                    first = (tt2 == 0)
                    kvdep = [e4p, f4p]
                    for hh in range(4):
                        hs = 128 * hh
                        for pi, (lt, rt) in enumerate(((khb, vhb), (khb, vlb), (klb, vhb))):
                            pb.op("tensor", lambda e, pp=pp2, hs=hs, lt=lt, rt=rt, hh=hh, first=first, pi=pi, tt=tt2: e.matmul(
                                pbk[4][:, hs:hs + 128], lt[pp][:, hs:hs + 128], rt[pp][:, hs:hs + 128],
                                start=(first and pi == 0 and hh == 0), stop=(tt == NT - 1 and pi == 2 and hh == 3)),
                                deps=(kvdep if (hh == 0 and pi == 0) else []), sig=False)
                        for pi, lt in enumerate((khb, klb)):
                            pb.op("tensor", lambda e, pp=pp2, hs=hs, lt=lt, hh=hh, first=first, pi=pi, tt=tt2: e.matmul(
                                pbk[5][:, hh:hh + 1], lt[pp][:, hs:hs + 128], ones_col_bf[:],
                                start=(first and pi == 0 and hh == 0), stop=(tt == NT - 1 and pi == 1 and hh == 3)),
                                deps=[], sig=(hh == 3 and pi == 1))
                    kv_tok[tt2] = pb.tok("tensor")

                for tt in range(NT):
                    pull_qo()
                    tbk_i, off = (tt * 128) // TBLK, (tt * 128) % TBLK
                    pp = tbk_i % 2
                    if off == 0:
                        lastrd = (tbk_i - 2) * (TBLK // 128) + (TBLK // 128) - 1
                        xdep = [xln_done] + ([ev_done[lastrd]] if lastrd in ev_done else
                                             ([prev_ev] if gi > 0 else [tbv]))
                        pb.dma(chQ, xbh[pp][:], xlnh[tbk_i], deps=xdep, slot=pp)
                        xtok = pb.dma(chQ, xbl[pp][:], xlnl[tbk_i], deps=[], slot=pp)
                    kbank, vbank = tt % 2, 2 + tt % 2
                    kdep = [xtok, tbv, tkh, tkl]
                    if tt >= 2:
                        kdep += [ev_done[tt - 2], stash[tt - 2][1], stash[tt - 2][2]]
                    first_mm = True
                    for ms in range(16):
                        xoff = TBLK * ms + off
                        chase = []
                        for pi, (lhs, rhs) in enumerate(((xbh, wkh), (xbh, wkl), (xbl, wkh))):
                            pb.op("tensor", lambda e, pp=pp, ms=ms, pi=pi, xoff=xoff, lhs=lhs, rhs=rhs, kbank=kbank: e.matmul(
                                pbk[kbank][:, 0:512], lhs[pp][:, xoff:xoff + 128], rhs[:, 512 * ms:512 * (ms + 1)],
                                start=(ms == 0 and pi == 0), stop=(ms == 15 and pi == 2)),
                                deps=((kdep if first_mm else []) + (chase if pi == 0 else [])), sig=(ms == 15 and pi == 2))
                            first_mm = False
                    tkmm = pb.tok("tensor")
                    for ms in range(16):
                        xoff = TBLK * ms + off
                        chase = [tvh, tvl] if (tt == 0 and ms == 0) else []
                        for pi, (lhs, rhs) in enumerate(((xbh, wvh), (xbh, wvl), (xbl, wvh))):
                            pb.op("tensor", lambda e, pp=pp, ms=ms, pi=pi, xoff=xoff, lhs=lhs, rhs=rhs, vbank=vbank: e.matmul(
                                pbk[vbank][:, 0:512], lhs[pp][:, xoff:xoff + 128], rhs[:, 512 * ms:512 * (ms + 1)],
                                start=(ms == 0 and pi == 0), stop=(ms == 15 and pi == 2)),
                                deps=(([] if ms or pi else [tkmm]) + (chase if pi == 0 else [])), sig=(ms == 15 and pi == 2))
                    tvmm = pb.tok("tensor")
                    ev_done[tt] = tvmm
                    qq = tt % 2
                    edep = [kv_tok[tt - 2]] if tt - 2 in kv_tok else ([prev_kv] if prev_kv else [])
                    e0 = pb.op("vector", lambda e, qq=qq, kbank=kbank: e.tensor_tensor(kf[qq][:], pbk[kbank][:, 0:512], bkb[:], ALU.add), deps=[tkmm])
                    e1 = pb.op("scalar", lambda e, qq=qq, tt=tt: e.activation(out=kf[qq][:], in_=kf[qq][:], func=AF.Relu, scale=dec_c[:, tt:tt + 1]), deps=[e0])
                    e2 = pb.op("vector", lambda e, qq=qq: e.tensor_copy(khb[qq][:], kf[qq][:]), deps=[e1] + edep)
                    e4 = pb.op("vector", lambda e, qq=qq: e.tensor_tensor(klb[qq][:], kf[qq][:], khb[qq][:], ALU.subtract), deps=[e2])
                    f0 = pb.op("vector", lambda e, qq=qq, vbank=vbank: e.tensor_tensor(vf[qq][:], pbk[vbank][:, 0:512], bvb[:], ALU.add), deps=[tvmm])
                    f2 = pb.op("vector", lambda e, qq=qq: e.tensor_copy(vhb[qq][:], vf[qq][:]), deps=[f0] + edep)
                    f4 = pb.op("vector", lambda e, qq=qq: e.tensor_tensor(vlb[qq][:], vf[qq][:], vhb[qq][:], ALU.subtract), deps=[f2])
                    stash[tt] = (qq, e4, f4)
                    if tt >= 1:
                        emit_kv(tt - 1)
                emit_kv(NT - 1)
                prev_ev = ev_done[NT - 1]
                prev_kv = kv_tok[NT - 1]
                prev_e4 = stash[NT - 1][1]
                tkv = pb.tok("tensor")
                for hh in range(4):
                    hg = 4 * gi + hh
                    pb.op("vector", lambda e, hh=hh, hg=hg: e.tensor_copy(kvbuf[:, 129 * hg:129 * hg + 128], pbk[4][:, 128 * hh:128 * (hh + 1)]), deps=[tkv])
                    pb.op("vector", lambda e, hh=hh, hg=hg: e.tensor_copy(kvbuf[:, 129 * hg + 128:129 * hg + 129], pbk[5][:, hh:hh + 1]), deps=[tkv])
            while gqo_live[0]:
                pull_qo()
        tkvl = pb.tok("vector")
        tci = pb.dma(chC, cc_kvi, kvbuf[:], deps=[tkvl])
        ckv = collective("AllReduce", ALU.add, cc_kvi, cc_kvo, deps=[tci])
        tco = pb.dma(chC, kvbuf[:], cc_kvo, deps=[ckv])
        tkv_ready = None
        a3_done = [tkvl, ("E", "tensor", pb.ecount["tensor"]), ("E", "scalar", pb.ecount["scalar"])]

        wq_done = [chPP.tok(s) for s in (0, 1, 2, 3)]
        wo_done = [chPP.tok(s) for s in (4, 5, 6, 7)]
        with contextlib.ExitStack() as c2:
            E2 = c2.enter_context
            xqh = [E2(sbuf_u(f"a4xh{i}", [128, 16 * QT], BF16)) for i in range(2)]
            xql = [E2(sbuf_u(f"a4xl{i}", [128, 16 * QT], BF16)) for i in range(2)]
            wqh = [E2(sbuf_u("a4wqh", [128, 2048], BF16)) for i in range(2)]
            wql = [E2(sbuf_u("a4wql", [128, 2048], BF16)) for i in range(2)]
            qf = [E2(sbuf_u("a4qf", [128, QT], F32)) for i in range(3)]
            qbh = [E2(sbuf_u("a4qbh", [128, QT], BF16)) for i in range(3)]
            qbl = [E2(sbuf_u("a4qbl", [128, QT], BF16)) for i in range(3)]
            mz = E2(sbuf_u("a4mz", [1, QT], F32))
            mbt = [E2(sbuf_u("a4mb", [128, CHQ], F32)) for i in range(2)]
            art = [E2(sbuf_u("a4ar", [128, CHQ], F32)) for i in range(2)]
            abh = E2(sbuf_u("a4abh", [128, 16 * QT], BF16))
            abl = E2(sbuf_u("a4abl", [128, 16 * QT], BF16))
            or_ = [E2(sbuf_u("a4o", [128, QT], F32)) for i in range(3)]
            prev_q = []
            prev_qtok = None
            prev_omm = None
            prev_og = None

            def load_xq(qq, deps0):
                pp = qq % 2
                q0 = QT * qq
                xqh3 = xqh[pp][:].rearrange("p (m q) -> p m q", m=16)
                xql3 = xql[pp][:].rearrange("p (m q) -> p m q", m=16)
                ldx = []
                start = q0
                while start < q0 + QT:
                    tbk2, off = start // TBLK, start % TBLK
                    ln = min(TBLK - off, q0 + QT - start)
                    cc = start - q0
                    ldx.append(pb.dma(chQ, xqh3[:, :, cc:cc + ln], xlnh[tbk2, :, :, off:off + ln], deps=(deps0 if not ldx else []), slot=pp))
                    ldx.append(pb.dma(chQ, xql3[:, :, cc:cc + ln], xlnl[tbk2, :, :, off:off + ln], deps=[], slot=pp))
                    start += ln
                return ldx[-1]
            xtok_next = load_xq(0, [xln_done] + a3_done)
            for qq in range(NQ):
                pp = qq % 2
                q0 = QT * qq
                decq = dec_r[:, q0:q0 + QT]
                xtok = xtok_next
                a_done = []
                hs_state = {}
                tail_toks = {}

                ztok = {}
                zmb = {}
                otok = {}

                def emit_z(f):
                    b, tq3, tqe = hs_state[f][:3]
                    zbank = 2 + (f % 2)
                    zdep = [tq3, tkv_ready] + ([ztok[f - 2]] if f - 2 in ztok else [])
                    tzc = None
                    for cq in range(NCQ):
                        for pi, (lt, rt) in enumerate(((kvh, qbh[b]), (kvh, qbl[b]), (kvl, qbh[b]))):
                            pb.op("tensor", lambda e, f=f, cq=cq, pi=pi, lt=lt, rt=rt, zbank=zbank: e.matmul(
                                pbk[zbank][0:1, 0:CHQ],
                                lt[:, 129 * f + 128:129 * f + 129], rt[:, CHQ * cq:CHQ * (cq + 1)],
                                start=(pi == 0), stop=(pi == 2)),
                                deps=(zdep if (cq == 0 and pi == 0) else []), sig=(pi == 2))
                        tzmm = pb.tok("tensor")
                        tzc = pb.op("vector", lambda e, cq=cq, zbank=zbank: e.tensor_copy(mz[:, CHQ * cq:CHQ * (cq + 1)], pbk[zbank][0:1, 0:CHQ]),
                                    deps=[tzmm] + ([zmb[f - 1]] if f - 1 in zmb else []))
                    m1_ = pb.op("vector", lambda e: e.tensor_tensor(mz[:], mz[:], decq, ALU.mult), deps=[tzc])
                    m2_ = pb.op("vector", lambda e: e.tensor_scalar(mz[:], mz[:], ATT_EPS, 0.0, ALU.add, ALU.add), deps=[m1_])
                    m3_ = pb.op("vector", lambda e: e.reciprocal(mz[:], mz[:]), deps=[m2_])
                    ztok[f] = pb.op("vector", lambda e: e.tensor_tensor(mz[:], mz[:], decq, ALU.mult), deps=[m3_])

                def emit_out(f):
                    b = hs_state[f][0]
                    m4_ = ztok[f]
                    for cq in range(NCQ):
                        obank = 4 + (f % 2)
                        odep = [m4_] + ([a_done[-2]] if len(a_done) >= 2 else [])
                        for pi, (lt, rt) in enumerate(((kvh, qbh[b]), (kvh, qbl[b]), (kvl, qbh[b]))):
                            pb.op("tensor", lambda e, f=f, cq=cq, pi=pi, lt=lt, rt=rt, obank=obank: e.matmul(
                                pbk[obank][:, 0:CHQ], lt[:, 129 * f:129 * f + 128], rt[:, CHQ * cq:CHQ * (cq + 1)],
                                start=(pi == 0), stop=(pi == 2)),
                                deps=(odep if pi == 0 else []), sig=(pi == 2))
                        tamm = pb.tok("tensor")
                        mbank = 6 + (f % 2)
                        tmb = pb.op("tensor", lambda e, cq=cq, mbank=mbank: e.matmul(
                            pbk[mbank][:, 0:CHQ], ones_row[:], mz[:, CHQ * cq:CHQ * (cq + 1)],
                            start=True, stop=True), deps=[m4_] + ([a_done[-2]] if len(a_done) >= 2 else []), sig=True)
                        zmb[f] = tmb
                        bb = f % 2
                        tmc = pb.op("vector", lambda e, bb=bb, mbank=mbank: e.tensor_copy(mbt[bb][:], pbk[mbank][:, 0:CHQ]), deps=[tmb])
                        tar = pb.op("vector", lambda e, bb=bb, obank=obank: e.tensor_tensor(art[bb][:], pbk[obank][:, 0:CHQ], mbt[bb][:], ALU.mult), deps=[tamm, tmc])
                        th1 = pb.op("vector", lambda e, bb=bb, f=f, cq=cq: e.tensor_copy(abh[:, QT * f + CHQ * cq:QT * f + CHQ * (cq + 1)], art[bb][:]), deps=[tar])
                        th3 = pb.op("vector", lambda e, bb=bb, f=f, cq=cq: e.tensor_tensor(abl[:, QT * f + CHQ * cq:QT * f + CHQ * (cq + 1)], art[bb][:], abh[:, QT * f + CHQ * cq:QT * f + CHQ * (cq + 1)], ALU.subtract), deps=[th1])
                        a_done.append(th3)
                    otok[f] = pb.tok("tensor")

                for f in range(16):
                    b = f % 2
                    b3 = f % 3
                    wdeps = [xtok] + wq_done + \
                        ([hs_state[f - 2][3]] if f - 2 in hs_state else
                         ([prev_omm[b]] if prev_omm else a3_done))
                    tlh = pb.dma(chW, wqh[b][:], wqhB2[f], deps=wdeps, slot=b)
                    tll = pb.dma(chW, wql[b][:], wqlB2[f], deps=[], slot=2 + b)
                    qbank = f % 2
                    qsp = [otok[f - 3]] if f - 3 in otok else []
                    mmdep = [tlh, tll, xtok] + qsp + \
                        ([hs_state[f - 2][2]] if f - 2 in hs_state else
                         ([prev_og[b]] if prev_og else []))
                    first_mm = True
                    for cq in range(NCQ):
                        for ms in range(16):
                            for pi, (lt, rt) in enumerate(((wqh, xqh[pp]), (wqh, xql[pp]), (wql, xqh[pp]))):
                                pb.op("tensor", lambda e, b=b, ms=ms, pi=pi, cq=cq, lt=lt, rt=rt, qbank=qbank: e.matmul(
                                    pbk[qbank][:, 0:CHQ],
                                    lt[b][:, 128 * ms:128 * (ms + 1)],
                                    rt[:, QT * ms + CHQ * cq:QT * ms + CHQ * (cq + 1)],
                                    start=(ms == 0 and pi == 0), stop=(ms == 15 and pi == 2)),
                                    deps=(mmdep if first_mm else []), sig=(ms == 15 and pi == 2))
                                first_mm = False
                    tqmm = pb.tok("tensor")
                    if f >= 3:
                        emit_out(f - 3)
                    if f >= 2:
                        emit_z(f - 2)
                    for cq in range(NCQ):
                        tqe = pb.op("scalar", lambda e, b3=b3, f=f, cq=cq, qbank=qbank: e.activation(
                            out=qf[b3][:, CHQ * cq:CHQ * (cq + 1)], in_=pbk[qbank][:, 0:CHQ], func=AF.Relu,
                            bias=bqc[:, f:f + 1]), deps=[tqmm] + ([hs_state[f - 3][1]] if f - 3 in hs_state else []))
                        tq1 = pb.op("vector", lambda e, b3=b3, cq=cq: e.tensor_copy(qbh[b3][:, CHQ * cq:CHQ * (cq + 1)], qf[b3][:, CHQ * cq:CHQ * (cq + 1)]), deps=[tqe] + qsp)
                        tq3 = pb.op("vector", lambda e, b3=b3, cq=cq: e.tensor_tensor(qbl[b3][:, CHQ * cq:CHQ * (cq + 1)], qf[b3][:, CHQ * cq:CHQ * (cq + 1)], qbh[b3][:, CHQ * cq:CHQ * (cq + 1)], ALU.subtract), deps=[tq1])
                    hs_state[f] = (b3, tq3, tqe, tqmm)
                    if tkv_ready is None:
                        s1 = pb.op("vector", lambda e: e.tensor_copy(kvh[:], kvbuf[:]), deps=[tco])
                        tkv_ready = pb.op("vector", lambda e: e.tensor_tensor(kvl[:], kvbuf[:], kvh[:], ALU.subtract), deps=[s1])
                prev_qtok = pb.tok("tensor")
                if qq + 1 < NQ:
                    xtok_next = load_xq(qq + 1, [prev_qtok])
                emit_out(13)
                emit_z(14)
                emit_out(14)
                emit_z(15)
                emit_out(15)
                o_mm = {}
                ost, og = {}, {}
                for dt in range(16):
                    b = dt % 2
                    wdeps = wo_done + ([o_mm[dt - 2]] if dt - 2 in o_mm else [hs_state[14 + b][3]])
                    tlh = pb.dma(chW, wqh[b][:], wohB2[dt], deps=wdeps, slot=b)
                    tll = pb.dma(chW, wql[b][:], wolB2[dt], deps=[], slot=2 + b)
                    for cq in range(NCQ):
                        obank = dt % 2
                        mmdep = [tlh, tll, a_done[-1]] + ([og[dt - 2]] if dt - 2 in og else [])
                        first_mm = True
                        for ms in range(16):
                            for pi, (lt, rt) in enumerate(((wqh, abh), (wqh, abl), (wql, abh))):
                                pb.op("tensor", lambda e, b=b, ms=ms, pi=pi, cq=cq, lt=lt, rt=rt, obank=obank: e.matmul(
                                    pbk[obank][:, 0:CHQ], lt[b][:, 128 * ms:128 * (ms + 1)],
                                    rt[:, QT * ms + CHQ * cq:QT * ms + CHQ * (cq + 1)],
                                    start=(ms == 0 and pi == 0), stop=(ms == 15 and pi == 2)),
                                    deps=(mmdep if first_mm else []), sig=(ms == 15 and pi == 2))
                                first_mm = False
                        tomm = pb.tok("tensor")
                        r = dt % 3
                        tg = pb.op("scalar", lambda e, r=r, dt=dt, obank=obank: e.activation(
                            out=or_[r][:, 0:CHQ], in_=pbk[obank][:, 0:CHQ], func=AF.Identity, bias=boc[:, dt:dt + 1]),
                            deps=[tomm] + ([ost[dt - 3]] if dt - 3 in ost else prev_q))
                        og[dt] = tg
                        ost[dt] = pb.dma(chO if dst_is_out else chS,
                                         dst_hbm[128 * dt:128 * (dt + 1), q0 + CHQ * cq:q0 + CHQ * (cq + 1)], or_[r][:, 0:CHQ],
                                         deps=[tg, tpre], slot=r, accum=ALU.add)
                    o_mm[dt] = pb.tok("tensor")
                prev_q = [ost[d] for d in (13, 14, 15)]
                prev_omm = (o_mm[14], o_mm[15])
                prev_og = (og[14], og[15])
        kv_ctx.close()
        pb.barrier()

    n_sub = len(parts)
    dsts = {}
    chain_src = x0
    for i, p in enumerate(parts):
        is_last = (i == n_sub - 1)
        dst = out if is_last else (xA if i == 0 else xB)
        if p.startswith("ffn"):
            ffn(chain_src, dst, p == "ffn1" or ("ffn1" not in parts), 0 if p == "ffn1" else 2, is_last)
        else:
            attn(chain_src, dst, is_last)
        chain_src = dst

    fwaits = []
    for c in pb.chans:
        fwaits += [(c.sems[s], 16 * c.cnt[s]) for s in range(len(c.sems)) if c.cnt[s]]
    pb.ops["sync"].append((lambda e: e.nop(), fwaits, []))

    with nc.Block() as block:
        pb.build(block)
    ctx.close()
    return nc



def _layout_inputs(L0_attn_x, Wq, bq, Wk, bk, Wv, bv, Wo, bo, w1, b1, w2, b2, T, ncores=NCORES):
    xt = np.ascontiguousarray(L0_attn_x[0].T)
    shared = {
        "wqT": np.ascontiguousarray(Wq.T), "wkT": np.ascontiguousarray(Wk.T),
        "wvT": np.ascontiguousarray(Wv.T), "woT": np.ascontiguousarray(Wo.T),
        "w1T": np.ascontiguousarray(w1.T), "w2T": np.ascontiguousarray(w2.T),
        "wqB": np.ascontiguousarray(Wq.T.reshape(16, 128, 16, 128).transpose(2, 1, 0, 3)),
        "woB": np.ascontiguousarray(Wo.T.reshape(16, 128, 16, 128).transpose(2, 1, 0, 3)),
        "wkB": np.ascontiguousarray(Wk.T.reshape(16, 128, 4, 512).transpose(2, 0, 1, 3)),
        "wvB": np.ascontiguousarray(Wv.T.reshape(16, 128, 4, 512).transpose(2, 0, 1, 3)),
        "w1W": np.ascontiguousarray(w1.T.reshape(16, 128, 8, 1024).transpose(2, 0, 1, 3)),
        "w2W": np.ascontiguousarray(w2.T.reshape(64, 128, 2, 1024).transpose(2, 0, 1, 3)),
        "bq_c": np.ascontiguousarray(bq.reshape(16, 128).T),
        "bo_c": np.ascontiguousarray(bo.reshape(16, 128).T),
        "bk_r": np.ascontiguousarray(bk.reshape(1, -1)),
        "bv_r": np.ascontiguousarray(bv.reshape(1, -1)),
        "b1_c": np.ascontiguousarray(b1.reshape(64, 128).T),
        "b2_c": np.ascontiguousarray(b2.reshape(16, 128).T),
    }
    in_maps = []
    for c in range(ncores):
        m = dict(shared)
        m["x"] = np.ascontiguousarray(xt[:, T * c:T * (c + 1)])
        m["w1s"] = np.ascontiguousarray(w1[(DF // ncores) * c:(DF // ncores) * (c + 1), :])
        m["w2s"] = np.ascontiguousarray(w2[(D // ncores) * c:(D // ncores) * (c + 1), :])
        pos = np.arange(T * c, T * (c + 1), dtype=np.float32)
        m["pos_c"] = np.ascontiguousarray(pos.reshape(-1, 128).T)
        m["pos_r"] = np.ascontiguousarray(pos.reshape(1, -1))
        in_maps.append(m)
    return in_maps


def kernel(L0_attn_x, L0_attn_Wq, L0_attn_bq, L0_attn_Wk, L0_attn_bk,
           L0_attn_Wv, L0_attn_bv, L0_attn_Wo, L0_attn_bo,
           L0_ffn_w1, L0_ffn_b1, L0_ffn_w2, L0_ffn_b2):
    T = SEQ // NCORES
    in_maps = _layout_inputs(np.asarray(L0_attn_x), np.asarray(L0_attn_Wq), np.asarray(L0_attn_bq),
                             np.asarray(L0_attn_Wk), np.asarray(L0_attn_bk),
                             np.asarray(L0_attn_Wv), np.asarray(L0_attn_bv),
                             np.asarray(L0_attn_Wo), np.asarray(L0_attn_bo),
                             np.asarray(L0_ffn_w1), np.asarray(L0_ffn_b1),
                             np.asarray(L0_ffn_w2), np.asarray(L0_ffn_b2), T)
    nc = build_nc(T=T)
    res = run_bass_kernel_spmd(nc, in_maps, core_ids=list(range(NCORES)))
    y = np.empty((1, SEQ, D), dtype=np.float32)
    for c in range(NCORES):
        y[0, T * c:T * (c + 1), :] = res.results[c]["out"].T
    return y


# version 30 (final)
# speedup vs baseline: 1.0300x; 1.0300x over previous
import contextlib
import numpy as np
import ml_dtypes
import concourse.bass as bass
import concourse.mybir as mybir
from concourse.bass_utils import run_bass_kernel_spmd

F32 = mybir.dt.float32
BF16 = mybir.dt.bfloat16
I32 = mybir.dt.int32
I8 = mybir.dt.int8
AF = mybir.ActivationFunctionType
ALU = mybir.AluOpType
AX = mybir.AxisListType

NCORES = 8
D = 2048
DF = 8192
H = 16
SEQ = 16384
LN_EPS = 1e-5
ATT_EPS = 1e-6
Q_EPS = 1e-8
QMAX = 127.0

ENGINES = ("sync", "gpsimd", "vector", "scalar", "tensor")


class Chan:

    def __init__(self, name, eng, sems):
        self.name, self.eng = name, eng
        self.sems = sems
        self.cnt = [0] * len(sems)

    def tok(self, slot=None):
        if slot is None:
            return [("C", self, s, self.cnt[s]) for s in range(len(self.sems))]
        return ("C", self, slot, self.cnt[slot])


class PB:

    def __init__(self, nc, ctx):
        self.nc = nc
        self.ops = {e: [] for e in ENGINES}
        self.esem = {e: ctx.enter_context(nc.semaphore(f"prog_{e}")) for e in ENGINES}
        self.ecount = {e: 0 for e in ENGINES}
        self.chans = []
        self.ctx = ctx
        self.barrier_toks = []

    def chan(self, name, eng, slots=1, nb=False):
        sems = [self.ctx.enter_context(self.nc.semaphore(f"ch_{name}{s}")) for s in range(slots)]
        c = Chan(name, eng, sems)
        c.nb = nb
        self.chans.append(c)
        return c

    def _w(self, dep):
        if dep is None:
            return None
        k = dep[0]
        if k == "E":
            _, eng, v = dep
            return (self.esem[eng], v) if v > 0 else None
        if k == "C":
            _, ch, s, n = dep
            return (ch.sems[s], 16 * n) if n > 0 else None
        _, sem, v = dep
        return (sem, v) if v > 0 else None

    def _waits_flat(self, deps):
        flat = []
        for d in deps:
            if isinstance(d, list):
                flat.extend(d)
            else:
                flat.append(d)
        return flat

    def _waits(self, deps):
        best = {}
        order = []
        for d in self._waits_flat(list(deps)) + self.barrier_toks:
            w = self._w(d)
            if w is not None:
                sem, val = w
                if id(sem) not in best:
                    order.append(sem)
                    best[id(sem)] = val
                else:
                    best[id(sem)] = max(best[id(sem)], val)
        return [(sem, best[id(sem)]) for sem in order]

    def op(self, eng, fn, deps=(), sig=True, incs=()):
        ii = list(incs)
        if sig:
            self.ecount[eng] += 1
            ii.append((self.esem[eng], 1))
        self.ops[eng].append((fn, self._waits(deps), ii))
        return ("E", eng, self.ecount[eng])

    def dma(self, ch, out, in_, deps=(), eng=None, slot=0, accum=None):
        ch.cnt[slot] += 1
        if accum is None:
            fn = lambda e, o=out, i=in_: e.dma_start(out=o, in_=i)
        else:
            fn = lambda e, o=out, i=in_, a=accum: e.dma_start(out=o, in_=i, accum_op=a)
        self.ops[eng or ch.eng].append((fn, self._waits(deps), [(ch.sems[slot], 16)]))
        return ch.tok(slot)

    def tok(self, eng):
        return ("E", eng, self.ecount[eng])

    def barrier(self):
        toks = [("E", e, self.ecount[e]) for e in ENGINES]
        for c in self.chans:
            if not getattr(c, "nb", False):
                toks.extend(c.tok())
        self.barrier_toks = toks

    def build(self, block):
        waited = {e: {} for e in ENGINES}

        def runner(ename):
            def run(eng):
                w8 = waited[ename]
                for fn, waits, incs in self.ops[ename]:
                    for sem, val in waits:
                        if w8.get(id(sem), 0) < val:
                            eng.wait_ge(sem, val)
                            w8[id(sem)] = val
                    inst = fn(eng)
                    for sem, amt in incs:
                        inst.then_inc(sem, amt)
            return run

        block.sync(runner("sync"))
        block.gpsimd(runner("gpsimd"))
        block.vector(runner("vector"))
        block.scalar(runner("scalar"))
        block.tensor(runner("tensor"))



def build_nc(T=2048, parts=("ffn1", "attn", "ffn2"), n_ranks=NCORES):
    nc = bass.Bass("TRN2", target_bir_lowering=False)
    NT = T // 128
    CH = min(512, T)
    NCH = T // CH
    QT = max(128, T // 4)
    NQ = T // QT
    CHQ = min(512, QT)
    NCQ = QT // CHQ
    TBLK = min(256, T)
    NTB = T // TBLK

    x0 = nc.dram_tensor("x", [D, T], F32, kind="ExternalInput").ap()
    wqT = nc.dram_tensor("wqT", [D, D], F32, kind="ExternalInput").ap()
    wkT = nc.dram_tensor("wkT", [D, D], F32, kind="ExternalInput").ap()
    wvT = nc.dram_tensor("wvT", [D, D], F32, kind="ExternalInput").ap()
    woT = nc.dram_tensor("woT", [D, D], F32, kind="ExternalInput").ap()
    wqB = nc.dram_tensor("wqB", [16, 128, D // 128, 128], F32, kind="ExternalInput").ap()
    wkB = nc.dram_tensor("wkB", [4, D // 128, 128, 512], F32, kind="ExternalInput").ap()
    wvB = nc.dram_tensor("wvB", [4, D // 128, 128, 512], F32, kind="ExternalInput").ap()
    w1W = nc.dram_tensor("w1W", [DF // 1024, D // 128, 128, 1024], F32, kind="ExternalInput").ap()
    w2W = nc.dram_tensor("w2W", [D // 1024, DF // 128, 128, 1024], F32, kind="ExternalInput").ap()
    woB = nc.dram_tensor("woB", [16, 128, D // 128, 128], F32, kind="ExternalInput").ap()
    w1T = nc.dram_tensor("w1T", [D, DF], F32, kind="ExternalInput").ap()
    w2T = nc.dram_tensor("w2T", [DF, D], F32, kind="ExternalInput").ap()
    w1s = nc.dram_tensor("w1s", [DF // n_ranks, D], F32, kind="ExternalInput").ap()
    w2s = nc.dram_tensor("w2s", [D // n_ranks, DF], F32, kind="ExternalInput").ap()
    bq_c = nc.dram_tensor("bq_c", [128, 16], F32, kind="ExternalInput").ap()
    bo_c = nc.dram_tensor("bo_c", [128, 16], F32, kind="ExternalInput").ap()
    bk_r = nc.dram_tensor("bk_r", [1, D], F32, kind="ExternalInput").ap()
    bv_r = nc.dram_tensor("bv_r", [1, D], F32, kind="ExternalInput").ap()
    b1_c = nc.dram_tensor("b1_c", [128, 64], F32, kind="ExternalInput").ap()
    b2_c = nc.dram_tensor("b2_c", [128, 16], F32, kind="ExternalInput").ap()
    pos_c = nc.dram_tensor("pos_c", [128, NT], F32, kind="ExternalInput").ap()
    pos_r = nc.dram_tensor("pos_r", [1, T], F32, kind="ExternalInput").ap()
    out = nc.dram_tensor("out", [D, T], F32, kind="ExternalOutput").ap()

    xA = nc.dram_tensor("xA", [D, T], F32).ap()
    xB = nc.dram_tensor("xB", [D, T], F32).ap()
    h_hbm = nc.dram_tensor("h_hbm", [NCH, DF // 128, 128, CH], F32).ap()
    qw1c = nc.dram_tensor("qw1c", [DF // 512, 128, D // 128, 512], BF16).ap()
    qw2c = nc.dram_tensor("qw2c", [D // 256, 128, DF // 128, 256], BF16).ap()
    xlnh = nc.dram_tensor("xlnh", [NTB, 128, D // 128, TBLK], BF16).ap()
    xlnl = nc.dram_tensor("xlnl", [NTB, 128, D // 128, TBLK], BF16).ap()
    xlnh_p = xlnh.rearrange("b p m f -> p b m f")
    xlnl_p = xlnl.rearrange("b p m f -> p b m f")
    cc_sw1i = nc.dram_tensor("cc_sw1i", [DF // n_ranks], F32).ap()
    cc_sw1o = nc.dram_tensor("cc_sw1o", [DF], F32).ap()
    cc_sw2i = nc.dram_tensor("cc_sw2i", [D // n_ranks], F32).ap()
    cc_sw2o = nc.dram_tensor("cc_sw2o", [D], F32).ap()
    cc_wui = nc.dram_tensor("cc_wui", [1, 1], F32).ap()
    cc_wuo = nc.dram_tensor("cc_wuo", [1, 1], F32).ap()
    cc_mxi = [nc.dram_tensor(f"cc_mxi{i}", [1, 1], F32).ap() for i in range(4)]
    cc_mxo = [nc.dram_tensor(f"cc_mxo{i}", [1, 1], F32).ap() for i in range(4)]
    cc_kvi = nc.dram_tensor("cc_kvi", [128, H * 129], F32).ap()
    cc_kvo = nc.dram_tensor("cc_kvo", [128, H * 129], F32).ap()
    wqhB2 = nc.dram_tensor("wqhB2", [16, 128, D], BF16).ap()
    wqlB2 = nc.dram_tensor("wqlB2", [16, 128, D], BF16).ap()
    wohB2 = nc.dram_tensor("wohB2", [16, 128, D], BF16).ap()
    wolB2 = nc.dram_tensor("wolB2", [16, 128, D], BF16).ap()
    rg = [list(range(n_ranks))]

    ctx = contextlib.ExitStack()
    E = ctx.enter_context
    pb = PB(nc, ctx)
    _uid = [0]
    _orig_sbuf = nc.sbuf_tensor

    def sbuf_u(name, shape, dt):
        _uid[0] += 1
        return _orig_sbuf(f"{name}_u{_uid[0]}", shape, dt)

    ones_col = E(nc.sbuf_tensor("ones_col", [128, 1], F32))
    ones_col_bf = E(nc.sbuf_tensor("ones_col_bf", [128, 1], BF16))
    ones_row = E(nc.sbuf_tensor("ones_row", [1, 128], F32))
    bqc = E(nc.sbuf_tensor("bqc", [128, 16], F32))
    boc = E(nc.sbuf_tensor("boc", [128, 16], F32))
    b1c = E(nc.sbuf_tensor("b1c", [128, 64], F32))
    b2ch = E(nc.sbuf_tensor("b2ch", [128, 16], F32))
    posc = E(nc.sbuf_tensor("posc", [128, NT], F32))
    dec_c = E(nc.sbuf_tensor("dec_c", [128, NT], F32))
    dec_r = E(nc.sbuf_tensor("dec_r", [1, T], F32))
    scw1c = E(nc.sbuf_tensor("scw1c", [128, 64], F32))
    scw2c = E(nc.sbuf_tensor("scw2c", [128, 16], F32))
    mb_ = E(nc.sbuf_tensor("mb_", [128, T], F32))
    rsb_ = E(nc.sbuf_tensor("rsb_", [128, T], F32))
    smax = E(nc.sbuf_tensor("smax", [128, 4], F32))
    smaxr = E(nc.sbuf_tensor("smaxr", [1, 128], F32))
    sval = E(nc.sbuf_tensor("sval", [1, 8], F32))
    srcol = E(nc.sbuf_tensor("srcol", [128, 2], F32))
    sxw1c = E(nc.sbuf_tensor("sxw1c", [128, 64], F32))
    sxw2c = E(nc.sbuf_tensor("sxw2c", [128, 16], F32))
    wpv = [E(nc.sbuf_tensor(f"wpv{i}", [128, 512], F32)) for i in range(2)]
    wph = [E(nc.sbuf_tensor(f"wph{i}", [128, 512], BF16)) for i in range(2)]
    wpl = [E(nc.sbuf_tensor(f"wpl{i}", [128, 512], BF16)) for i in range(2)]
    pbk = [E(nc.psum_tensor(f"pbk{i}", [128, 512], F32)) for i in range(8)]

    cc_sem = E(nc.semaphore("cc_sem"))
    cc_n = [0]

    chW = pb.chan("W", "sync", slots=4)
    chA = pb.chan("A", "scalar", slots=3)
    chH = pb.chan("Hh", "sync", slots=6)
    chQ = pb.chan("Q", "scalar", slots=2)
    chRX = pb.chan("Rx", "scalar", slots=3)
    chS = pb.chan("S", "gpsimd", slots=4)
    chM = pb.chan("M", "sync")
    chM2 = pb.chan("M2", "sync")
    chL1 = pb.chan("L1", "sync")
    chAR = pb.chan("Ar", "sync")
    chR = pb.chan("R", "sync")
    chC = pb.chan("Cc", "gpsimd")
    chWB = pb.chan("Wb", "scalar", slots=4)
    chPW = pb.chan("Pw", "sync", slots=2)
    chO = pb.chan("O", "gpsimd", slots=3)
    chPL = pb.chan("Pl", "sync", slots=4, nb=True)
    chXS = pb.chan("Xs", "sync", slots=2)
    chPP = pb.chan("Pp", "sync", slots=8, nb=True)

    def hl_prep(slabs, chunk):
        tsp, stt = {}, {}
        for i, (src, dh, dl, sl) in enumerate(slabs):
            b = i % 2
            tl = pb.dma(chPL, wpv[b][:], src, deps=([tsp[i - 2]] if i >= 2 else []), slot=i % 4)
            th = pb.op("vector", lambda e, b=b: e.tensor_copy(wph[b][:], wpv[b][:]),
                       deps=[tl] + (stt[i - 2] if i - 2 in stt else []))
            tsp[i] = pb.op("vector", lambda e, b=b: e.tensor_tensor(wpl[b][:], wpv[b][:], wph[b][:], ALU.subtract), deps=[th])
            s1 = pb.dma(chPP, dh, wph[b][:], deps=[tsp[i]], slot=sl + 2 * b)
            s2 = pb.dma(chPP, dl, wpl[b][:], deps=[tsp[i]], slot=sl + 2 * b + 1)
            stt[i] = [s1, s2]
            if (i + 1) % chunk == 0 and (i + 1) < len(slabs):
                yield
        yield

    def collective(kind, op_, in_ap, out_ap, deps):
        cc_n[0] += 1
        pb.op("gpsimd",
              lambda e: e.collective_compute(kind, op_, replica_groups=rg,
                                             ins=[in_ap.opt()], outs=[out_ap.opt()]),
              deps=deps, sig=False, incs=[(cc_sem, 1)])
        return ("RAW", cc_sem, cc_n[0])

    t_wu0 = pb.op("vector", lambda e: e.memset(sval[:, 7:8], 0.0))
    t_wu1 = pb.dma(chC, cc_wui, sval[:, 7:8], deps=[t_wu0])
    collective("AllReduce", ALU.max, cc_wui, cc_wuo, deps=[t_wu1])
    for dst, srcap in ((bqc, bq_c), (boc, bo_c),
                       (b1c, b1_c), (b2ch, b2_c), (posc, pos_c), (dec_r, pos_r)):
        pb.dma(chM, dst[:], srcap)
    consts = chM.tok()
    pb.op("vector", lambda e: e.memset(ones_col[:], 1.0))
    pb.op("vector", lambda e: e.memset(ones_col_bf[:], 1.0))
    pb.op("vector", lambda e: e.memset(ones_row[:], 1.0))
    pb.op("vector", lambda e: e.tensor_scalar(b2ch[:], b2ch[:], 0.5, 0.0, ALU.mult, ALU.add), deps=[consts])
    pb.op("scalar", lambda e: e.activation(out=dec_c[:], in_=posc[:], func=AF.Exp, scale=-1.0 / SEQ), deps=[consts])
    pb.op("scalar", lambda e: e.activation(out=dec_r[:], in_=dec_r[:], func=AF.Exp, scale=-1.0 / SEQ), deps=[consts])

    with contextlib.ExitStack() as c2:
        E2 = c2.enter_context
        wsl = E2(nc.sbuf_tensor("wsl", [128, DF], F32))
        n1 = (DF // n_ranks) // 128
        n2 = (D // n_ranks) // 128
        wscol = E2(nc.sbuf_tensor("wscol", [128, n1 + n2], F32))
        swt = E2(nc.sbuf_tensor("swt", [64, 128], F32))
        n1 = (DF // n_ranks) // 128
        tprev, trtok = None, []
        for j in range(n1):
            tl = pb.dma(chW, wsl[:, 0:D], w1s[128 * j:128 * (j + 1), :], deps=[tprev] if tprev else [])
            tprev = pb.op("vector", lambda e, j=j: e.tensor_reduce(
                wscol[:, j:j + 1], wsl[:, 0:D], axis=AX.X, op=ALU.max, apply_absolute_value=True), deps=[tl])
            trtok.append(tprev)
        sts = [pb.dma(chC, cc_sw1i[128 * j:128 * (j + 1)], wscol[:, j:j + 1], deps=[trtok[j]]) for j in range(n1)]
        cc1 = collective("AllGather", ALU.bypass, cc_sw1i, cc_sw1o, deps=sts)
        tprev2, trtok2 = None, []
        for j in range(n2):
            tl = pb.dma(chW, wsl[:], w2s[128 * j:128 * (j + 1), :], deps=[tprev2 or trtok[-1]])
            tprev2 = pb.op("vector", lambda e, j=j: e.tensor_reduce(
                wscol[:, n1 + j:n1 + j + 1], wsl[:], axis=AX.X, op=ALU.max, apply_absolute_value=True), deps=[tl])
            trtok2.append(tprev2)
        sts2 = [pb.dma(chC, cc_sw2i[128 * j:128 * (j + 1)], wscol[:, n1 + j:n1 + j + 1], deps=[trtok2[j]]) for j in range(n2)]
        cc2 = collective("AllGather", ALU.bypass, cc_sw2i, cc_sw2o, deps=sts2)
        t_l1 = pb.dma(chL1, swt[:], cc_sw1o.rearrange("(c p) -> c p", p=128), deps=[cc1])
        tt1 = [pb.op("vector", lambda e, i=i, j=j: e.transpose(
            scw1c[32 * j:32 * j + 32, 32 * i:32 * i + 32], swt[32 * i:32 * i + 32, 32 * j:32 * j + 32]),
            deps=[t_l1]) for i in range(2) for j in range(4)]
        tt2 = [pb.dma(chM2, scw2c[:, j:j + 1], cc_sw2o[128 * j:128 * (j + 1)], deps=[cc2]) for j in range(16)]
        t_sc1 = pb.op("vector", lambda e: e.tensor_scalar(scw1c[:], scw1c[:], 1.0 / QMAX, Q_EPS, ALU.mult, ALU.max), deps=tt1)
        t_sc2 = pb.op("vector", lambda e: e.tensor_scalar(scw2c[:], scw2c[:], 1.0 / QMAX, Q_EPS, ALU.mult, ALU.max), deps=[tt2[-1]])
    pb.barrier()


    def ln_stats(src_hbm):
        with contextlib.ExitStack() as c2:
            E2 = c2.enter_context
            slab = [E2(sbuf_u(f"lns{i}", [128, T], F32)) for i in range(3)]
            sq = E2(sbuf_u("lnsq", [128, T], F32))
            sc2 = E2(sbuf_u("lnsc", [1, 4 * T], F32))
            mrow, rsrow, s1row, s2row = (sc2[:, i * T:(i + 1) * T] for i in range(4))
            tsq, tmark = None, {}
            for ft in range(16):
                b = ft % 3
                deps = [tmark[ft - 3]] if ft >= 3 else []
                tl = pb.dma(chA, slab[b][:], src_hbm[128 * ft:128 * (ft + 1), :], deps=deps, slot=b)
                tsq = pb.op("scalar", lambda e, b=b: e.activation(out=sq[:], in_=slab[b][:], func=AF.Square),
                            deps=[tl, tmark.get(ft - 1)] if ft >= 1 else [tl])
                for c in range(NCH):
                    pb.op("tensor", lambda e, b=b, c=c, ft=ft: e.matmul(
                        pbk[c][0:1, 0:CH], ones_col[:], slab[b][:, CH * c:CH * (c + 1)],
                        start=(ft == 0), stop=(ft == 15)), deps=[tl], sig=False)
                for c in range(NCH):
                    pb.op("tensor", lambda e, c=c, ft=ft: e.matmul(
                        pbk[NCH + c][0:1, 0:CH], ones_col[:], sq[:, CH * c:CH * (c + 1)],
                        start=(ft == 0), stop=(ft == 15)), deps=[tsq], sig=(c == NCH - 1))
                tmark[ft] = pb.tok("tensor")
            tmm = pb.tok("tensor")
            tm = []
            for c in range(NCH):
                tm.append(pb.op("vector", lambda e, c=c: e.tensor_scalar(
                    mrow[:, CH * c:CH * (c + 1)], pbk[c][0:1, 0:CH], 1.0 / D, 0.0, ALU.mult, ALU.add), deps=[tmm]))
                tm.append(pb.op("vector", lambda e, c=c: e.tensor_scalar(
                    s1row[:, CH * c:CH * (c + 1)], pbk[NCH + c][0:1, 0:CH], 1.0 / D, 0.0, ALU.mult, ALU.add), deps=[tmm]))
            t1 = pb.op("vector", lambda e: e.tensor_tensor(s2row[:], mrow[:], mrow[:], ALU.mult), deps=tm)
            t2 = pb.op("vector", lambda e: e.scalar_tensor_tensor(s1row[:], s1row[:], LN_EPS, s2row[:], ALU.add, ALU.subtract), deps=[t1])
            t3 = pb.op("scalar", lambda e: e.activation(out=s2row[:], in_=s1row[:], func=AF.Sqrt), deps=[t2])
            t4 = pb.op("vector", lambda e: e.reciprocal(rsrow[:], s2row[:]), deps=[t3])
            outs = []
            for c in range(NCH):
                tbm = pb.op("tensor", lambda e, c=c: e.matmul(
                    pbk[c][:, 0:CH], ones_row[:], mrow[:, CH * c:CH * (c + 1)], start=True, stop=True),
                    deps=[t4] + tm, sig=True)
                outs.append(pb.op("vector", lambda e, c=c: e.tensor_copy(mb_[:, CH * c:CH * (c + 1)], pbk[c][:, 0:CH]), deps=[tbm]))
                tbr = pb.op("tensor", lambda e, c=c: e.matmul(
                    pbk[NCH + c][:, 0:CH], ones_row[:], rsrow[:, CH * c:CH * (c + 1)], start=True, stop=True),
                    deps=[t4] + tm, sig=True)
                outs.append(pb.op("vector", lambda e, c=c: e.tensor_copy(rsb_[:, CH * c:CH * (c + 1)], pbk[NCH + c][:, 0:CH]), deps=[tbr]))
            return outs

    def scalar_allreduce_max(vslot, cc_idx, dep, filler=None):
        tsh = pb.dma(chR, smaxr[0:1, :], smax[:, vslot:vslot + 1], deps=[dep])
        tg = pb.op("vector", lambda e: e.tensor_reduce(sval[:, 4 * vslot:4 * vslot + 1] if False else sval[:, 0:1], smaxr[:], axis=AX.X, op=ALU.max), deps=[tsh])
        tci = pb.dma(chC, cc_mxi[cc_idx], sval[:, 0:1], deps=[tg])
        ctk = collective("AllReduce", ALU.max, cc_mxi[cc_idx], cc_mxo[cc_idx], deps=[tci])
        tco = pb.dma(chC, sval[:, 0:1], cc_mxo[cc_idx], deps=[ctk])
        if filler:
            filler()
        ts = pb.op("vector", lambda e: e.tensor_scalar(sval[:, 1:2], sval[:, 0:1], 1.0 / QMAX, Q_EPS, ALU.mult, ALU.max), deps=[tco])
        tr = pb.op("vector", lambda e: e.reciprocal(sval[:, 2:3], sval[:, 1:2]), deps=[ts])
        tb = pb.op("tensor", lambda e: e.matmul(pbk[7][:, 0:2], ones_row[:], sval[:, 1:3], start=True, stop=True), deps=[tr], sig=True)
        return pb.op("vector", lambda e: e.tensor_copy(srcol[:], pbk[7][:, 0:2]), deps=[tb])

    def quant_pass(src_hbm, dst_cb, stats_dep, with_scale):
        with contextlib.ExitStack() as c2:
            E2 = c2.enter_context
            slab = [E2(sbuf_u(f"qp{i}", [128, T], F32)) for i in range(2)]
            dt_ = E2(sbuf_u("qpd", [128, T], F32))
            it_ = E2(sbuf_u("qpi", [128, T], F32))
            tprev = pb.op("vector", lambda e: e.memset(smax[:, 0:1], 0.0), deps=stats_dep) if not with_scale else None
            tend = {}
            for ft in range(16):
                b = ft % 2
                tl = pb.dma(chA, slab[b][:], src_hbm[128 * ft:128 * (ft + 1), :],
                            deps=[tend[ft - 2]] if ft >= 2 else list(stats_dep), slot=b)
                d0 = [tl] + stats_dep + ([tprev] if tprev else [])
                ta = pb.op("vector", lambda e, b=b: e.tensor_tensor(dt_[:], slab[b][:], mb_[:], ALU.subtract), deps=d0)
                tb = pb.op("vector", lambda e: e.tensor_tensor(dt_[:], dt_[:], rsb_[:], ALU.mult), deps=[ta])
                if with_scale:
                    ti = pb.op("vector", lambda e: e.tensor_scalar(it_[:].bitcast(I32), dt_[:], srcol[:, 1:2], 0.0, ALU.mult, ALU.add), deps=[tb])
                    tprev = dst_cb(ft, it_, [ti])
                else:
                    tr = pb.op("vector", lambda e: e.tensor_reduce(smax[:, 1:2], dt_[:], axis=AX.X, op=ALU.max, apply_absolute_value=True), deps=[tb, tprev] if tprev else [tb])
                    tprev = pb.op("vector", lambda e: e.tensor_tensor(smax[:, 0:1], smax[:, 0:1], smax[:, 1:2], ALU.max), deps=[tr])
                tend[ft] = tprev
            return tprev, tend

    def wprep(wsrcW, ab_row, nin, nout, cache, blksz, tdep, bufs, blk_tok):
        FW = 1024
        ws, ab, yt, qtw, arow = bufs
        prev_w = []
        ty_t, tq_t = {}, {}
        nms = nin // 128
        for fw in range(nout // FW):
            f0 = FW * fw
            t_r = pb.dma(chAR, ab[:], ab_row[f0:f0 + FW].unsqueeze(0).broadcast_to([128, FW]), deps=list(tdep) + prev_w)
            t_s = pb.op("vector", lambda e: e.tensor_scalar(ab[:], ab[:], 1.0 / QMAX, Q_EPS, ALU.mult, ALU.max), deps=[t_r])
            t_ab = pb.op("vector", lambda e: e.reciprocal(ab[:], ab[:]), deps=[t_s])
            GMS = min(nms, 16)
            qv = qtw[:, 0:GMS * FW].rearrange("p (m f) -> p m f", m=GMS)
            sts = []
            for ms in range(nms):
                b4, b2 = ms % 4, ms % 2
                tl = pb.dma(chW, ws[b4][:], wsrcW[fw, ms], deps=[ty_t[ms - 3]] if ms >= 3 else [t_ab], slot=b4)
                ty_t[ms] = pb.op("vector", lambda e, b4=b4, b2=b2: e.tensor_tensor(yt[b2][:].bitcast(I32), ws[b4][:], ab[:], ALU.mult),
                                 deps=[tl, t_ab] + ([tq_t[ms - 2]] if ms >= 2 else prev_w))
                tq_t[ms] = pb.op("vector", lambda e, b2=b2, ms=ms: e.tensor_copy(qtw[:, FW * (ms % GMS):FW * (ms % GMS + 1)], yt[b2][:].bitcast(I32)),
                                 deps=[ty_t[ms]] + ([sts[-1]] if (ms % GMS == 0 and sts) else []))
                if (ms + 1) % GMS == 0:
                    g0 = ms - GMS + 1
                    for blk in range(FW // blksz):
                        gblk = (f0 + blksz * blk) // blksz
                        sts.append(pb.dma(chPW, cache[gblk, :, g0:g0 + GMS, :], qv[:, :, blksz * blk:blksz * (blk + 1)],
                                          deps=[tq_t[ms]], slot=fw % 2))
                    if ms + 1 < nms:
                        yield
            for blk in range(FW // blksz):
                blk_tok[(f0 + blksz * blk) // blksz] = [sts[-1]]
            prev_w = [sts[-1]]
            blk_tok["last"] = prev_w
            yield

    def ffn(src_hbm, dst_hbm, first, cc0, dst_is_out):
        tpre = pb.dma(chO if dst_is_out else chS, dst_hbm, src_hbm, slot=0)
        stats = ln_stats(src_hbm)
        pb.barrier()
        tmax, _ = quant_pass(src_hbm, None, stats, False)
        qx_ctx = contextlib.ExitStack()
        qx = qx_ctx.enter_context(sbuf_u("qx", [128, 16 * T], BF16))
        bt1, bt2 = {}, {}
        if first:
            FW = 1024
            E3 = qx_ctx.enter_context
            wpbufs = ([E3(sbuf_u("wp", [128, FW], F32)) for i in range(4)],
                      E3(sbuf_u("wpa", [128, FW], F32)),
                      [E3(sbuf_u("wpy", [128, FW], F32)) for i in range(2)],
                      E3(sbuf_u("wpq", [128, 16 * FW], BF16)),
                      None)
            NW1 = DF // 1024
            g1 = wprep(w1W, cc_sw1o, D, DF, qw1c, 512, [tmax], wpbufs, bt1)
            g1n = 2
            g2 = None

        def fill_w01():
            if first:
                next(g1)
                next(g1)
        tsr = scalar_allreduce_max(0, cc0, tmax, filler=fill_w01)
        tsx1 = pb.op("vector", lambda e: e.tensor_scalar(sxw1c[:], scw1c[:], srcol[:, 0:1], 0.0, ALU.mult, ALU.add), deps=[tsr])

        def qx_store(ft, it_, deps):
            return pb.op("vector", lambda e, ft=ft: e.tensor_copy(qx[:, T * ft:T * (ft + 1)], it_[:].bitcast(I32)), deps=deps)
        tqx, qx_t = quant_pass(src_hbm, qx_store, [tsr], True)
        pb.barrier()

        with contextlib.ExitStack() as c2:
            E2 = c2.enter_context
            wb = [E2(sbuf_u("p5w", [128, 16 * 512], BF16)) for i in range(2)]
            ht = [E2(sbuf_u("p5h", [128, CH], F32)) for i in range(4)]
            tz = pb.op("vector", lambda e: e.memset(smax[:, 2:3], 0.0))
            hm_prev = tz
            ev_act, ev_st = {}, {}
            NB = DF // 512
            for fb in range(NB):
                if first:
                    if g1n < NW1:
                        next(g1)
                        g1n += 1
                    else:
                        if g2 is None:
                            g2 = wprep(w2W, cc_sw2o, DF, D, qw2c, 256,
                                       bt1["last"], wpbufs, bt2)
                        next(g2, None)
                b = fb % 2
                i0 = fb * 4 * NCH
                wdep = list(bt1[fb]) if fb in bt1 else []
                if fb >= 2:
                    wdep.append(ev_act[4 * NCH * (fb - 1)])
                tw = pb.dma(chWB, wb[b][:, 0:8 * 512], qw1c[fb, :, 0:8, :], deps=wdep, slot=2 * b)
                twB = pb.dma(chWB, wb[b][:, 8 * 512:16 * 512], qw1c[fb, :, 8:16, :], deps=wdep, slot=2 * b + 1)
                for f in range(4):
                    for tc in range(NCH):
                        j = i0 + 4 * f + tc if False else i0 + NCH * f + tc
                        bank = j % 6
                        mmdep = [tw] + ([qx_t[0]] if fb == 0 else [tqx]) + \
                            ([ev_act[j - 6]] if j - 6 in ev_act else [])
                        for ms in range(16):
                            chase = [qx_t[ms]] if fb == 0 else []
                            pb.op("tensor", lambda e, b=b, f=f, ms=ms, tc=tc, bank=bank: e.matmul(
                                pbk[bank][:, 0:CH],
                                wb[b][:, 512 * ms + 128 * f:512 * ms + 128 * (f + 1)],
                                qx[:, T * ms + CH * tc:T * ms + CH * (tc + 1)],
                                start=(ms == 0), stop=(ms == 15)),
                                deps=(mmdep if ms == 0 else []) + chase + ([twB] if ms == 8 else []), sig=(ms == 15))
                        tmm = pb.tok("tensor")
                        fidx = fb * 4 + f
                        hb = j % 4
                        hdep = [ev_st.get(j - 4)] if j - 4 in ev_st else []
                        tg = pb.op("scalar", lambda e, hb=hb, bank=bank, fidx=fidx: e.activation(
                            out=ht[hb][:], in_=pbk[bank][:, 0:CH], func=AF.Gelu,
                            bias=b1c[:, fidx:fidx + 1], scale=sxw1c[:, fidx:fidx + 1]),
                            deps=[tmm, tsx1] + hdep)
                        trm = pb.op("vector", lambda e, hb=hb: e.tensor_reduce(
                            smax[:, 3:4], ht[hb][:], axis=AX.X, op=ALU.max, apply_absolute_value=True), deps=[tg, hm_prev])
                        hm_prev = pb.op("vector", lambda e: e.tensor_tensor(
                            smax[:, 2:3], smax[:, 2:3], smax[:, 3:4], ALU.max), deps=[trm, hm_prev])
                        ev_st[j] = pb.dma(chS, h_hbm[tc, fidx], ht[hb][:], deps=[trm], slot=j % 4)
                        ev_act[j] = tg
        qx_ctx.close()
        pb.barrier()
        tsr2 = scalar_allreduce_max(2, cc0 + 1, hm_prev)
        tsxa = pb.op("vector", lambda e: e.tensor_scalar(sxw2c[:], scw2c[:], srcol[:, 0:1], 0.0, ALU.mult, ALU.add), deps=[tsr2])
        tsx2 = pb.op("vector", lambda e: e.tensor_scalar(sxw2c[:], sxw2c[:], 0.5, 0.0, ALU.mult, ALU.add), deps=[tsxa])
        hst_done = chS.tok()

        with contextlib.ExitStack() as c2:
            E2 = c2.enter_context
            qh = [E2(sbuf_u("p6qh", [128, 64 * CH], BF16)) for i in range(2)]
            hb_ = [E2(sbuf_u("p6h", [128, CH], F32)) for i in range(5)]
            qsc = [E2(sbuf_u("p6qs", [128, CH], F32)) for i in range(3)]
            w2b = [E2(sbuf_u("p6w", [128, 16 * 256], BF16)) for i in range(2)]
            ot_ = [E2(sbuf_u("p6o", [128, CH], F32)) for i in range(3)]
            NF2 = DF // 128
            tg_tok = {}
            st_tok = {}
            qdone = {}
            mmtok = {}
            sub_tok = {}
            for tc in range(NCH):
                qb_ = tc % 2
                tqh = {}
                qfree = [mmtok[tc - 2]] if tc - 2 in mmtok else []
                for f2 in range(NF2):
                    r = f2 % 5
                    r3 = f2 % 3
                    ldep = [hst_done] + ([tqh[f2 - 5]] if f2 >= 5 else ([qdone[tc - 1]] if tc - 1 in qdone else []))
                    tl = pb.dma(chH, hb_[r][:], h_hbm[tc, f2], deps=ldep, slot=r)
                    ty = pb.op("vector", lambda e, r=r, r3=r3: e.tensor_scalar(qsc[r3][:].bitcast(I32), hb_[r][:], srcol[:, 1:2], 0.0, ALU.mult, ALU.add),
                               deps=[tl, tsr2] + (qfree if f2 == 0 else []))
                    tq = pb.op("vector", lambda e, r3=r3, f2=f2, qb_=qb_: e.tensor_copy(qh[qb_][:, CH * f2:CH * (f2 + 1)], qsc[r3][:].bitcast(I32)), deps=[ty])
                    tqh[f2] = tq
                qdone[tc] = pb.tok("vector")
                for db in range(D // 256):
                    bank0, bank1 = (2 * db) % 4, (2 * db + 1) % 4
                    for sub in range(4):
                        b = (4 * db + sub) % 2
                        k = 4 * (NCH * 0 + db) + sub + 32 * tc
                        wdep = ([sub_tok[k - 2]] if k - 2 in sub_tok else []) + \
                               (bt2[db] if (tc == 0 and sub == 0 and db in bt2) else [])
                        tw1 = pb.dma(chWB, w2b[b][:, 0:8 * 256], qw2c[db, :, 16 * sub:16 * sub + 8, :], deps=wdep, slot=2 * b)
                        tw2 = pb.dma(chWB, w2b[b][:, 8 * 256:16 * 256], qw2c[db, :, 16 * sub + 8:16 * (sub + 1), :], deps=wdep, slot=2 * b + 1)
                        for d2 in range(2):
                            dt = 2 * db + d2
                            bank = bank0 if d2 == 0 else bank1
                            pdt = dt - 4 if dt >= 4 else dt + 12
                            mmdep = [tw1, qdone[tc], tsx2] + ([tg_tok[pdt]] if pdt in tg_tok else [])
                            for msl in range(16):
                                ms = 16 * sub + msl
                                pb.op("tensor", lambda e, b=b, msl=msl, ms=ms, d2=d2, bank=bank, qb_=qb_: e.matmul(
                                    pbk[bank][:, 0:CH],
                                    w2b[b][:, 256 * msl + 128 * d2:256 * msl + 128 * (d2 + 1)],
                                    qh[qb_][:, CH * ms:CH * (ms + 1)],
                                    start=(ms == 0), stop=(ms == NF2 - 1)),
                                    deps=(mmdep if msl == 0 else []) + ([tw2] if msl == 8 else []), sig=(msl == 15))
                        sub_tok[k] = pb.tok("tensor")
                    tmm = pb.tok("tensor")
                    for d2 in range(2):
                        dt = 2 * db + d2
                        bank = bank0 if d2 == 0 else bank1
                        r = dt % 3
                        pst = dt - 3 if dt >= 3 else (15, 13, 14)[dt]
                        tg = pb.op("scalar", lambda e, r=r, bank=bank, dt=dt: e.activation(
                            out=ot_[r][:], in_=pbk[bank][:, 0:CH], func=AF.Identity,
                            bias=b2ch[:, dt:dt + 1], scale=sxw2c[:, dt:dt + 1]),
                            deps=[tmm] + ([st_tok[pst]] if pst in st_tok else []))
                        tg_tok[dt] = tg
                        st_tok[dt] = pb.dma(chO if dst_is_out else chS,
                                            dst_hbm[128 * dt:128 * (dt + 1), CH * tc:CH * (tc + 1)], ot_[r][:],
                                            deps=[tg, tpre], slot=r, accum=ALU.add)
                mmtok[tc] = pb.tok("tensor")
        pb.barrier()

    def attn(src_hbm, dst_hbm, dst_is_out):
        tpre = pb.dma(chO if dst_is_out else chS, dst_hbm, src_hbm, slot=0)
        kv_ctx = contextlib.ExitStack()
        kvbuf = kv_ctx.enter_context(sbuf_u("kvbuf", [128, H * 129], F32))
        kvh = kv_ctx.enter_context(sbuf_u("kvh", [128, H * 129], BF16))
        kvl = kv_ctx.enter_context(sbuf_u("kvl", [128, H * 129], BF16))
        stats = ln_stats(src_hbm)
        pb.barrier()
        with contextlib.ExitStack() as c2:
            E2 = c2.enter_context
            slab = [E2(sbuf_u(f"a2s{i}", [128, T], F32)) for i in range(2)]
            dt_ = E2(sbuf_u("a2d", [128, T], F32))
            hti = E2(sbuf_u("a2h", [128, T], BF16))
            lti = E2(sbuf_u("a2l", [128, T], BF16))
            tend = {}
            for ft in range(16):
                b = ft % 2
                tl = pb.dma(chA, slab[b][:], src_hbm[128 * ft:128 * (ft + 1), :],
                            deps=[tend[ft - 2]] if ft >= 2 else [], slot=b)
                prev = [tend[ft - 1]] if ft >= 1 else []
                ta = pb.op("vector", lambda e, b=b: e.tensor_tensor(dt_[:], slab[b][:], mb_[:], ALU.subtract), deps=[tl] + stats + prev)
                tb = pb.op("vector", lambda e: e.tensor_tensor(dt_[:], dt_[:], rsb_[:], ALU.mult), deps=[ta])
                th = pb.op("vector", lambda e: e.tensor_copy(hti[:], dt_[:]), deps=[tb])
                tl2 = pb.op("vector", lambda e: e.tensor_tensor(lti[:], dt_[:], hti[:], ALU.subtract), deps=[th])
                pb.dma(chXS, xlnh_p[:, :, ft, :], hti[:], deps=[th], slot=ft % 2)
                pb.dma(chXS, xlnl_p[:, :, ft, :], lti[:], deps=[tl2], slot=ft % 2)
                tend[ft] = chXS.tok(ft % 2)
        xln_done = chXS.tok()
        pb.barrier()

        with contextlib.ExitStack() as c2:
            E2 = c2.enter_context
            wkh = E2(sbuf_u("wkh", [128, 16 * 512], BF16))
            wkl = E2(sbuf_u("wkl", [128, 16 * 512], BF16))
            wvh = E2(sbuf_u("wvh", [128, 16 * 512], BF16))
            wvl = E2(sbuf_u("wvl", [128, 16 * 512], BF16))
            wtmp = [E2(sbuf_u(f"a3w{i}", [128, 512], F32)) for i in range(2)]
            wd = E2(sbuf_u("a3wd", [128, 512], F32))
            xbh = [E2(sbuf_u(f"a3xh{i}", [128, 16 * TBLK], BF16)) for i in range(2)]
            xbl = [E2(sbuf_u(f"a3xl{i}", [128, 16 * TBLK], BF16)) for i in range(2)]
            kf = [E2(sbuf_u(f"a3kf{i}", [128, 512], F32)) for i in range(2)]
            khb = [E2(sbuf_u(f"a3kh{i}", [128, 512], BF16)) for i in range(2)]
            klb = [E2(sbuf_u(f"a3kl{i}", [128, 512], BF16)) for i in range(2)]
            vf = [E2(sbuf_u(f"a3vf{i}", [128, 512], F32)) for i in range(2)]
            vhb = [E2(sbuf_u(f"a3vh{i}", [128, 512], BF16)) for i in range(2)]
            vlb = [E2(sbuf_u(f"a3vl{i}", [128, 512], BF16)) for i in range(2)]
            bkb = E2(sbuf_u("a3bkb", [128, 512], F32))
            bvb = E2(sbuf_u("a3bvb", [128, 512], F32))
            bkvr = E2(sbuf_u("a3bkvr", [1, 1024], F32))
            wqB_f = wqB.rearrange("f p m c -> f p (m c)")
            woB_f = woB.rearrange("f p m c -> f p (m c)")
            qo_slabs = []
            for (wsrcB, dh2, dl2, sl) in ((wqB_f, wqhB2, wqlB2, 0), (woB_f, wohB2, wolB2, 4)):
                for f in range(16):
                    for c in range(4):
                        qo_slabs.append((wsrcB[f][:, 512 * c:512 * (c + 1)],
                                         dh2[f][:, 512 * c:512 * (c + 1)],
                                         dl2[f][:, 512 * c:512 * (c + 1)], sl))
            gqo = hl_prep(qo_slabs, 2)
            gqo_live = [True]

            def pull_qo():
                if gqo_live[0] and next(gqo, "END") == "END":
                    gqo_live[0] = False
            prev_ev, prev_kv, prev_e4 = None, None, None
            for gi in range(4):
                gs = 512 * gi
                tspl = {}
                for w_i, (wsrc, bufh, bufl) in enumerate(((wkB, wkh, wkl), (wvB, wvh, wvl))):
                    for ms in range(16):
                        b = ms % 2
                        k = w_i * 16 + ms
                        tl = pb.dma(chW, wtmp[b][:], wsrc[gi, ms],
                                    deps=[tspl[k - 2]] if k >= 2 else ([prev_ev] if gi > 0 else []), slot=k % 4)
                        t1 = pb.op("vector", lambda e, b=b, ms=ms, bufh=bufh: e.tensor_copy(bufh[:, 512 * ms:512 * (ms + 1)], wtmp[b][:]), deps=[tl] + ([tspl[k - 1]] if k >= 1 else []))
                        tspl[k] = pb.op("vector", lambda e, b=b, ms=ms, bufh=bufh, bufl=bufl: e.tensor_tensor(bufl[:, 512 * ms:512 * (ms + 1)], wtmp[b][:], bufh[:, 512 * ms:512 * (ms + 1)], ALU.subtract), deps=[t1])
                tw_done = tspl[31]
                tbl1 = pb.dma(chM, bkvr[:, 0:512], bk_r[:, gs:gs + 512], deps=[prev_ev] if gi > 0 else [])
                tbl2 = pb.dma(chM, bkvr[:, 512:1024], bv_r[:, gs:gs + 512], deps=[])
                tb1 = pb.op("tensor", lambda e: e.matmul(pbk[6][:, 0:512], ones_row[:], bkvr[:, 0:512], start=True, stop=True), sig=True,
                            deps=[tspl[0], tbl1, tbl2] + ([prev_ev, prev_e4] if gi > 0 else []))
                tbk = pb.op("vector", lambda e: e.tensor_copy(bkb[:], pbk[6][:, 0:512]), deps=[tb1])
                tb2 = pb.op("tensor", lambda e: e.matmul(pbk[7][:, 0:512], ones_row[:], bkvr[:, 512:1024], start=True, stop=True), sig=True, deps=[tbk])
                tbv = pb.op("vector", lambda e: e.tensor_copy(bvb[:], pbk[7][:, 0:512]), deps=[tb2])
                ev_done = {}
                stash = {}
                kv_tok = {}

                def emit_kv(tt2):
                    pp2, e4p, f4p = stash[tt2]
                    first = (tt2 == 0)
                    kvdep = [e4p, f4p]
                    for hh in range(4):
                        hs = 128 * hh
                        for pi, (lt, rt) in enumerate(((khb, vhb), (khb, vlb), (klb, vhb))):
                            pb.op("tensor", lambda e, pp=pp2, hs=hs, lt=lt, rt=rt, hh=hh, first=first, pi=pi, tt=tt2: e.matmul(
                                pbk[4][:, hs:hs + 128], lt[pp][:, hs:hs + 128], rt[pp][:, hs:hs + 128],
                                start=(first and pi == 0 and hh == 0), stop=(tt == NT - 1 and pi == 2 and hh == 3)),
                                deps=(kvdep if (hh == 0 and pi == 0) else []), sig=False)
                        for pi, lt in enumerate((khb, klb)):
                            pb.op("tensor", lambda e, pp=pp2, hs=hs, lt=lt, hh=hh, first=first, pi=pi, tt=tt2: e.matmul(
                                pbk[5][:, hh:hh + 1], lt[pp][:, hs:hs + 128], ones_col_bf[:],
                                start=(first and pi == 0 and hh == 0), stop=(tt == NT - 1 and pi == 1 and hh == 3)),
                                deps=[], sig=(hh == 3 and pi == 1))
                    kv_tok[tt2] = pb.tok("tensor")

                for tt in range(NT):
                    pull_qo()
                    tbk_i, off = (tt * 128) // TBLK, (tt * 128) % TBLK
                    pp = tbk_i % 2
                    if off == 0:
                        lastrd = (tbk_i - 2) * (TBLK // 128) + (TBLK // 128) - 1
                        xdep = [xln_done] + ([ev_done[lastrd]] if lastrd in ev_done else
                                             ([prev_ev] if gi > 0 else [tbv]))
                        pb.dma(chQ, xbh[pp][:], xlnh[tbk_i], deps=xdep, slot=pp)
                        xtok = pb.dma(chQ, xbl[pp][:], xlnl[tbk_i], deps=[], slot=pp)
                    kbank, vbank = tt % 2, 2 + tt % 2
                    kdep = [xtok, tbv]
                    if tt >= 2:
                        kdep += [ev_done[tt - 2], stash[tt - 2][1], stash[tt - 2][2]]
                    first_mm = True
                    for ms in range(16):
                        xoff = TBLK * ms + off
                        chase = [tspl[ms]]
                        for pi, (lhs, rhs) in enumerate(((xbh, wkh), (xbh, wkl), (xbl, wkh))):
                            pb.op("tensor", lambda e, pp=pp, ms=ms, pi=pi, xoff=xoff, lhs=lhs, rhs=rhs, kbank=kbank: e.matmul(
                                pbk[kbank][:, 0:512], lhs[pp][:, xoff:xoff + 128], rhs[:, 512 * ms:512 * (ms + 1)],
                                start=(ms == 0 and pi == 0), stop=(ms == 15 and pi == 2)),
                                deps=((kdep if first_mm else []) + (chase if pi == 0 else [])), sig=(ms == 15 and pi == 2))
                            first_mm = False
                    tkmm = pb.tok("tensor")
                    for ms in range(16):
                        xoff = TBLK * ms + off
                        chase = [tspl[16 + ms]]
                        for pi, (lhs, rhs) in enumerate(((xbh, wvh), (xbh, wvl), (xbl, wvh))):
                            pb.op("tensor", lambda e, pp=pp, ms=ms, pi=pi, xoff=xoff, lhs=lhs, rhs=rhs, vbank=vbank: e.matmul(
                                pbk[vbank][:, 0:512], lhs[pp][:, xoff:xoff + 128], rhs[:, 512 * ms:512 * (ms + 1)],
                                start=(ms == 0 and pi == 0), stop=(ms == 15 and pi == 2)),
                                deps=(([] if ms or pi else [tkmm]) + (chase if pi == 0 else [])), sig=(ms == 15 and pi == 2))
                    tvmm = pb.tok("tensor")
                    ev_done[tt] = tvmm
                    qq = tt % 2
                    edep = [kv_tok[tt - 2]] if tt - 2 in kv_tok else ([prev_kv] if prev_kv else [])
                    e0 = pb.op("vector", lambda e, qq=qq, kbank=kbank: e.tensor_tensor(kf[qq][:], pbk[kbank][:, 0:512], bkb[:], ALU.add), deps=[tkmm])
                    e1 = pb.op("scalar", lambda e, qq=qq, tt=tt: e.activation(out=kf[qq][:], in_=kf[qq][:], func=AF.Relu, scale=dec_c[:, tt:tt + 1]), deps=[e0])
                    e2 = pb.op("vector", lambda e, qq=qq: e.tensor_copy(khb[qq][:], kf[qq][:]), deps=[e1] + edep)
                    e4 = pb.op("vector", lambda e, qq=qq: e.tensor_tensor(klb[qq][:], kf[qq][:], khb[qq][:], ALU.subtract), deps=[e2])
                    f0 = pb.op("vector", lambda e, qq=qq, vbank=vbank: e.tensor_tensor(vf[qq][:], pbk[vbank][:, 0:512], bvb[:], ALU.add), deps=[tvmm])
                    f2 = pb.op("vector", lambda e, qq=qq: e.tensor_copy(vhb[qq][:], vf[qq][:]), deps=[f0] + edep)
                    f4 = pb.op("vector", lambda e, qq=qq: e.tensor_tensor(vlb[qq][:], vf[qq][:], vhb[qq][:], ALU.subtract), deps=[f2])
                    stash[tt] = (qq, e4, f4)
                    if tt >= 1:
                        emit_kv(tt - 1)
                emit_kv(NT - 1)
                prev_ev = ev_done[NT - 1]
                prev_kv = kv_tok[NT - 1]
                prev_e4 = stash[NT - 1][1]
                tkv = pb.tok("tensor")
                for hh in range(4):
                    hg = 4 * gi + hh
                    pb.op("vector", lambda e, hh=hh, hg=hg: e.tensor_copy(kvbuf[:, 129 * hg:129 * hg + 128], pbk[4][:, 128 * hh:128 * (hh + 1)]), deps=[tkv])
                    pb.op("vector", lambda e, hh=hh, hg=hg: e.tensor_copy(kvbuf[:, 129 * hg + 128:129 * hg + 129], pbk[5][:, hh:hh + 1]), deps=[tkv])
            while gqo_live[0]:
                pull_qo()
        tkvl = pb.tok("vector")
        tci = pb.dma(chC, cc_kvi, kvbuf[:], deps=[tkvl])
        ckv = collective("AllReduce", ALU.add, cc_kvi, cc_kvo, deps=[tci])
        tco = pb.dma(chC, kvbuf[:], cc_kvo, deps=[ckv])
        tkv_ready = None
        a3_done = [tkvl, ("E", "tensor", pb.ecount["tensor"]), ("E", "scalar", pb.ecount["scalar"])]

        wq_done = [chPP.tok(s) for s in (0, 1, 2, 3)]
        wo_done = [chPP.tok(s) for s in (4, 5, 6, 7)]
        with contextlib.ExitStack() as c2:
            E2 = c2.enter_context
            xqh = [E2(sbuf_u(f"a4xh{i}", [128, 16 * QT], BF16)) for i in range(2)]
            xql = [E2(sbuf_u(f"a4xl{i}", [128, 16 * QT], BF16)) for i in range(2)]
            wqh = [E2(sbuf_u("a4wqh", [128, 2048], BF16)) for i in range(2)]
            wql = [E2(sbuf_u("a4wql", [128, 2048], BF16)) for i in range(2)]
            qf = [E2(sbuf_u("a4qf", [128, QT], F32)) for i in range(3)]
            qbh = [E2(sbuf_u("a4qbh", [128, QT], BF16)) for i in range(3)]
            qbl = [E2(sbuf_u("a4qbl", [128, QT], BF16)) for i in range(3)]
            mz = E2(sbuf_u("a4mz", [1, QT], F32))
            mbt = [E2(sbuf_u("a4mb", [128, CHQ], F32)) for i in range(2)]
            art = [E2(sbuf_u("a4ar", [128, CHQ], F32)) for i in range(2)]
            abh = E2(sbuf_u("a4abh", [128, 16 * QT], BF16))
            abl = E2(sbuf_u("a4abl", [128, 16 * QT], BF16))
            or_ = [E2(sbuf_u("a4o", [128, QT], F32)) for i in range(3)]
            prev_q = []
            prev_qtok = None
            prev_omm = None
            prev_og = None

            def load_xq(qq, deps0):
                pp = qq % 2
                q0 = QT * qq
                xqh3 = xqh[pp][:].rearrange("p (m q) -> p m q", m=16)
                xql3 = xql[pp][:].rearrange("p (m q) -> p m q", m=16)
                ldx = []
                start = q0
                while start < q0 + QT:
                    tbk2, off = start // TBLK, start % TBLK
                    ln = min(TBLK - off, q0 + QT - start)
                    cc = start - q0
                    ldx.append(pb.dma(chQ, xqh3[:, :, cc:cc + ln], xlnh[tbk2, :, :, off:off + ln], deps=(deps0 if not ldx else []), slot=pp))
                    ldx.append(pb.dma(chQ, xql3[:, :, cc:cc + ln], xlnl[tbk2, :, :, off:off + ln], deps=[], slot=pp))
                    start += ln
                return ldx[-1]
            xtok_next = load_xq(0, [xln_done] + a3_done)
            for qq in range(NQ):
                pp = qq % 2
                q0 = QT * qq
                decq = dec_r[:, q0:q0 + QT]
                xtok = xtok_next
                a_done = []
                hs_state = {}
                tail_toks = {}

                ztok = {}
                zmb = {}
                otok = {}

                def emit_z(f):
                    b, tq3, tqe = hs_state[f][:3]
                    zbank = 2 + (f % 2)
                    zdep = [tq3, tkv_ready] + ([ztok[f - 2]] if f - 2 in ztok else [])
                    tzc = None
                    for cq in range(NCQ):
                        for pi, (lt, rt) in enumerate(((kvh, qbh[b]), (kvh, qbl[b]), (kvl, qbh[b]))):
                            pb.op("tensor", lambda e, f=f, cq=cq, pi=pi, lt=lt, rt=rt, zbank=zbank: e.matmul(
                                pbk[zbank][0:1, 0:CHQ],
                                lt[:, 129 * f + 128:129 * f + 129], rt[:, CHQ * cq:CHQ * (cq + 1)],
                                start=(pi == 0), stop=(pi == 2)),
                                deps=(zdep if (cq == 0 and pi == 0) else []), sig=(pi == 2))
                        tzmm = pb.tok("tensor")
                        tzc = pb.op("vector", lambda e, cq=cq, zbank=zbank: e.tensor_copy(mz[:, CHQ * cq:CHQ * (cq + 1)], pbk[zbank][0:1, 0:CHQ]),
                                    deps=[tzmm] + ([zmb[f - 1]] if f - 1 in zmb else []))
                    m1_ = pb.op("vector", lambda e: e.tensor_tensor(mz[:], mz[:], decq, ALU.mult), deps=[tzc])
                    m2_ = pb.op("vector", lambda e: e.tensor_scalar(mz[:], mz[:], ATT_EPS, 0.0, ALU.add, ALU.add), deps=[m1_])
                    m3_ = pb.op("vector", lambda e: e.reciprocal(mz[:], mz[:]), deps=[m2_])
                    ztok[f] = pb.op("vector", lambda e: e.tensor_tensor(mz[:], mz[:], decq, ALU.mult), deps=[m3_])

                def emit_out(f):
                    b = hs_state[f][0]
                    m4_ = ztok[f]
                    for cq in range(NCQ):
                        obank = 4 + (f % 2)
                        odep = [m4_] + ([a_done[-2]] if len(a_done) >= 2 else [])
                        for pi, (lt, rt) in enumerate(((kvh, qbh[b]), (kvh, qbl[b]), (kvl, qbh[b]))):
                            pb.op("tensor", lambda e, f=f, cq=cq, pi=pi, lt=lt, rt=rt, obank=obank: e.matmul(
                                pbk[obank][:, 0:CHQ], lt[:, 129 * f:129 * f + 128], rt[:, CHQ * cq:CHQ * (cq + 1)],
                                start=(pi == 0), stop=(pi == 2)),
                                deps=(odep if pi == 0 else []), sig=(pi == 2))
                        tamm = pb.tok("tensor")
                        mbank = 6 + (f % 2)
                        tmb = pb.op("tensor", lambda e, cq=cq, mbank=mbank: e.matmul(
                            pbk[mbank][:, 0:CHQ], ones_row[:], mz[:, CHQ * cq:CHQ * (cq + 1)],
                            start=True, stop=True), deps=[m4_] + ([a_done[-2]] if len(a_done) >= 2 else []), sig=True)
                        zmb[f] = tmb
                        bb = f % 2
                        tmc = pb.op("vector", lambda e, bb=bb, mbank=mbank: e.tensor_copy(mbt[bb][:], pbk[mbank][:, 0:CHQ]), deps=[tmb])
                        tar = pb.op("vector", lambda e, bb=bb, obank=obank: e.tensor_tensor(art[bb][:], pbk[obank][:, 0:CHQ], mbt[bb][:], ALU.mult), deps=[tamm, tmc])
                        th1 = pb.op("vector", lambda e, bb=bb, f=f, cq=cq: e.tensor_copy(abh[:, QT * f + CHQ * cq:QT * f + CHQ * (cq + 1)], art[bb][:]), deps=[tar])
                        th3 = pb.op("vector", lambda e, bb=bb, f=f, cq=cq: e.tensor_tensor(abl[:, QT * f + CHQ * cq:QT * f + CHQ * (cq + 1)], art[bb][:], abh[:, QT * f + CHQ * cq:QT * f + CHQ * (cq + 1)], ALU.subtract), deps=[th1])
                        a_done.append(th3)
                    otok[f] = pb.tok("tensor")

                for f in range(16):
                    b = f % 2
                    b3 = f % 3
                    wdeps = [xtok] + wq_done + \
                        ([hs_state[f - 2][3]] if f - 2 in hs_state else
                         ([prev_omm[b]] if prev_omm else a3_done))
                    tlh = pb.dma(chW, wqh[b][:], wqhB2[f], deps=wdeps, slot=b)
                    tll = pb.dma(chW, wql[b][:], wqlB2[f], deps=[], slot=2 + b)
                    qbank = f % 2
                    qsp = [otok[f - 3]] if f - 3 in otok else []
                    mmdep = [tlh, tll, xtok] + qsp + \
                        ([hs_state[f - 2][2]] if f - 2 in hs_state else
                         ([prev_og[b]] if prev_og else []))
                    first_mm = True
                    for cq in range(NCQ):
                        for ms in range(16):
                            for pi, (lt, rt) in enumerate(((wqh, xqh[pp]), (wqh, xql[pp]), (wql, xqh[pp]))):
                                pb.op("tensor", lambda e, b=b, ms=ms, pi=pi, cq=cq, lt=lt, rt=rt, qbank=qbank: e.matmul(
                                    pbk[qbank][:, 0:CHQ],
                                    lt[b][:, 128 * ms:128 * (ms + 1)],
                                    rt[:, QT * ms + CHQ * cq:QT * ms + CHQ * (cq + 1)],
                                    start=(ms == 0 and pi == 0), stop=(ms == 15 and pi == 2)),
                                    deps=(mmdep if first_mm else []), sig=(ms == 15 and pi == 2))
                                first_mm = False
                    tqmm = pb.tok("tensor")
                    if f >= 3:
                        emit_out(f - 3)
                    if f >= 2:
                        emit_z(f - 2)
                    for cq in range(NCQ):
                        tqe = pb.op("scalar", lambda e, b3=b3, f=f, cq=cq, qbank=qbank: e.activation(
                            out=qf[b3][:, CHQ * cq:CHQ * (cq + 1)], in_=pbk[qbank][:, 0:CHQ], func=AF.Relu,
                            bias=bqc[:, f:f + 1]), deps=[tqmm] + ([hs_state[f - 3][1]] if f - 3 in hs_state else []))
                        tq1 = pb.op("vector", lambda e, b3=b3, cq=cq: e.tensor_copy(qbh[b3][:, CHQ * cq:CHQ * (cq + 1)], qf[b3][:, CHQ * cq:CHQ * (cq + 1)]), deps=[tqe] + qsp)
                        tq3 = pb.op("vector", lambda e, b3=b3, cq=cq: e.tensor_tensor(qbl[b3][:, CHQ * cq:CHQ * (cq + 1)], qf[b3][:, CHQ * cq:CHQ * (cq + 1)], qbh[b3][:, CHQ * cq:CHQ * (cq + 1)], ALU.subtract), deps=[tq1])
                    hs_state[f] = (b3, tq3, tqe, tqmm)
                    if tkv_ready is None:
                        s1 = pb.op("vector", lambda e: e.tensor_copy(kvh[:], kvbuf[:]), deps=[tco])
                        tkv_ready = pb.op("vector", lambda e: e.tensor_tensor(kvl[:], kvbuf[:], kvh[:], ALU.subtract), deps=[s1])
                prev_qtok = pb.tok("tensor")
                if qq + 1 < NQ:
                    xtok_next = load_xq(qq + 1, [prev_qtok])
                emit_out(13)
                emit_z(14)
                emit_out(14)
                emit_z(15)
                emit_out(15)
                o_mm = {}
                ost, og = {}, {}
                for dt in range(16):
                    b = dt % 2
                    wdeps = wo_done + ([o_mm[dt - 2]] if dt - 2 in o_mm else [hs_state[14 + b][3]])
                    tlh = pb.dma(chW, wqh[b][:], wohB2[dt], deps=wdeps, slot=b)
                    tll = pb.dma(chW, wql[b][:], wolB2[dt], deps=[], slot=2 + b)
                    for cq in range(NCQ):
                        obank = dt % 2
                        mmdep = [tlh, tll, a_done[-1]] + ([og[dt - 2]] if dt - 2 in og else [])
                        first_mm = True
                        for ms in range(16):
                            for pi, (lt, rt) in enumerate(((wqh, abh), (wqh, abl), (wql, abh))):
                                pb.op("tensor", lambda e, b=b, ms=ms, pi=pi, cq=cq, lt=lt, rt=rt, obank=obank: e.matmul(
                                    pbk[obank][:, 0:CHQ], lt[b][:, 128 * ms:128 * (ms + 1)],
                                    rt[:, QT * ms + CHQ * cq:QT * ms + CHQ * (cq + 1)],
                                    start=(ms == 0 and pi == 0), stop=(ms == 15 and pi == 2)),
                                    deps=(mmdep if first_mm else []), sig=(ms == 15 and pi == 2))
                                first_mm = False
                        tomm = pb.tok("tensor")
                        r = dt % 3
                        tg = pb.op("scalar", lambda e, r=r, dt=dt, obank=obank: e.activation(
                            out=or_[r][:, 0:CHQ], in_=pbk[obank][:, 0:CHQ], func=AF.Identity, bias=boc[:, dt:dt + 1]),
                            deps=[tomm] + ([ost[dt - 3]] if dt - 3 in ost else prev_q))
                        og[dt] = tg
                        ost[dt] = pb.dma(chO if dst_is_out else chS,
                                         dst_hbm[128 * dt:128 * (dt + 1), q0 + CHQ * cq:q0 + CHQ * (cq + 1)], or_[r][:, 0:CHQ],
                                         deps=[tg, tpre], slot=r, accum=ALU.add)
                    o_mm[dt] = pb.tok("tensor")
                prev_q = [ost[d] for d in (13, 14, 15)]
                prev_omm = (o_mm[14], o_mm[15])
                prev_og = (og[14], og[15])
        kv_ctx.close()
        pb.barrier()

    n_sub = len(parts)
    dsts = {}
    chain_src = x0
    for i, p in enumerate(parts):
        is_last = (i == n_sub - 1)
        dst = out if is_last else (xA if i == 0 else xB)
        if p.startswith("ffn"):
            ffn(chain_src, dst, p == "ffn1" or ("ffn1" not in parts), 0 if p == "ffn1" else 2, is_last)
        else:
            attn(chain_src, dst, is_last)
        chain_src = dst

    fwaits = []
    for c in pb.chans:
        fwaits += [(c.sems[s], 16 * c.cnt[s]) for s in range(len(c.sems)) if c.cnt[s]]
    pb.ops["sync"].append((lambda e: e.nop(), fwaits, []))

    with nc.Block() as block:
        pb.build(block)
    ctx.close()
    return nc



def _layout_inputs(L0_attn_x, Wq, bq, Wk, bk, Wv, bv, Wo, bo, w1, b1, w2, b2, T, ncores=NCORES):
    xt = np.ascontiguousarray(L0_attn_x[0].T)
    shared = {
        "wqT": np.ascontiguousarray(Wq.T), "wkT": np.ascontiguousarray(Wk.T),
        "wvT": np.ascontiguousarray(Wv.T), "woT": np.ascontiguousarray(Wo.T),
        "w1T": np.ascontiguousarray(w1.T), "w2T": np.ascontiguousarray(w2.T),
        "wqB": np.ascontiguousarray(Wq.T.reshape(16, 128, 16, 128).transpose(2, 1, 0, 3)),
        "woB": np.ascontiguousarray(Wo.T.reshape(16, 128, 16, 128).transpose(2, 1, 0, 3)),
        "wkB": np.ascontiguousarray(Wk.T.reshape(16, 128, 4, 512).transpose(2, 0, 1, 3)),
        "wvB": np.ascontiguousarray(Wv.T.reshape(16, 128, 4, 512).transpose(2, 0, 1, 3)),
        "w1W": np.ascontiguousarray(w1.T.reshape(16, 128, 8, 1024).transpose(2, 0, 1, 3)),
        "w2W": np.ascontiguousarray(w2.T.reshape(64, 128, 2, 1024).transpose(2, 0, 1, 3)),
        "bq_c": np.ascontiguousarray(bq.reshape(16, 128).T),
        "bo_c": np.ascontiguousarray(bo.reshape(16, 128).T),
        "bk_r": np.ascontiguousarray(bk.reshape(1, -1)),
        "bv_r": np.ascontiguousarray(bv.reshape(1, -1)),
        "b1_c": np.ascontiguousarray(b1.reshape(64, 128).T),
        "b2_c": np.ascontiguousarray(b2.reshape(16, 128).T),
    }
    in_maps = []
    for c in range(ncores):
        m = dict(shared)
        m["x"] = np.ascontiguousarray(xt[:, T * c:T * (c + 1)])
        m["w1s"] = np.ascontiguousarray(w1[(DF // ncores) * c:(DF // ncores) * (c + 1), :])
        m["w2s"] = np.ascontiguousarray(w2[(D // ncores) * c:(D // ncores) * (c + 1), :])
        pos = np.arange(T * c, T * (c + 1), dtype=np.float32)
        m["pos_c"] = np.ascontiguousarray(pos.reshape(-1, 128).T)
        m["pos_r"] = np.ascontiguousarray(pos.reshape(1, -1))
        in_maps.append(m)
    return in_maps


def kernel(L0_attn_x, L0_attn_Wq, L0_attn_bq, L0_attn_Wk, L0_attn_bk,
           L0_attn_Wv, L0_attn_bv, L0_attn_Wo, L0_attn_bo,
           L0_ffn_w1, L0_ffn_b1, L0_ffn_w2, L0_ffn_b2):
    T = SEQ // NCORES
    in_maps = _layout_inputs(np.asarray(L0_attn_x), np.asarray(L0_attn_Wq), np.asarray(L0_attn_bq),
                             np.asarray(L0_attn_Wk), np.asarray(L0_attn_bk),
                             np.asarray(L0_attn_Wv), np.asarray(L0_attn_bv),
                             np.asarray(L0_attn_Wo), np.asarray(L0_attn_bo),
                             np.asarray(L0_ffn_w1), np.asarray(L0_ffn_b1),
                             np.asarray(L0_ffn_w2), np.asarray(L0_ffn_b2), T)
    nc = build_nc(T=T)
    res = run_bass_kernel_spmd(nc, in_maps, core_ids=list(range(NCORES)))
    y = np.empty((1, SEQ, D), dtype=np.float32)
    for c in range(NCORES):
        y[0, T * c:T * (c + 1), :] = res.results[c]["out"].T
    return y
